# version 11
# baseline (speedup 1.0000x reference)
import numpy as np
import ml_dtypes
import contextlib
import concourse.bass as bass
import concourse.mybir as mybir
from concourse.bass_utils import run_bass_kernel_spmd

F32 = mybir.dt.float32
BF16 = mybir.dt.bfloat16
FP8 = mybir.dt.float8e4
I32 = mybir.dt.int32
ALU = mybir.AluOpType
AF = mybir.ActivationFunctionType
AX = mybir.AxisListType.X

PI = float(np.float32(np.pi))
PIH = float(np.float32(np.pi / 2))
C2PI_HI = float(np.float32(2 * np.pi))
C2PI_LO = float(np.float64(2 * np.pi) - np.float64(np.float32(2 * np.pi)))
PIH64 = float(np.pi / 2)
PISAFE = float(np.nextafter(np.float32(np.pi), np.float32(0)))
ZCLIP = float(np.float64(1e-6) * 128)

D = 1024
NCH = 8
INNER = 4096
ICH = 32
S = 2048
NSUB = 16
H = 16
FD = 128
HD = 64
NSEQ = 2

ENGINES = ("sync", "tensor", "vector", "scalar", "gpsimd")
DEBUG_TAPS = False
DEBUG_TAP_GROUP = 0


def _iv(ap):
    t = ap.tensor
    off = int(ap.offset)
    span = 1
    for p in ap.ap:
        st, cnt = int(p[0]), int(p[1])
        span += (cnt - 1) * abs(st)
    return (t.name, off, off + span)


class PB:
    def __init__(self, nc, es):
        self.nc = nc
        self.q = {e: [] for e in ENGINES}
        self.sem = {}
        self.cnt = {}
        for name in ("DL", "DS", "GD", "TE", "VE", "SE", "GE"):
            self.sem[name] = es.enter_context(nc.semaphore(name))
            self.cnt[name] = 0
        self.esem = {"tensor": "TE", "vector": "VE", "scalar": "SE", "gpsimd": "GE"}
        self.dirty = {e: ([], []) for e in ENGINES}
        self.last = {}

    def _emit(self, eng, fn):
        self.q[eng].append(fn)

    def _track(self, eng, reads, writes):
        if eng in ("sync", "tensor"):
            return
        R, W = self.dirty[eng]
        rv = [_iv(a) for a in reads]
        wv = [_iv(a) for a in writes]

        def ovl(a, b):
            return a[0] == b[0] and a[1] < b[2] and b[1] < a[2]
        if any(ovl(r, w) for r in rv for w in W) or any(ovl(w, x) for w in wv for x in W + R):
            self._emit(eng, lambda e: e.drain())
            self.dirty[eng] = ([], [])
            R, W = self.dirty[eng]
        R.extend(rv)
        W.extend(wv)

    def op(self, eng, fn, reads=(), writes=(), inc=None, skip_track=False):
        if not skip_track:
            self._track(eng, list(reads), list(writes))
        if inc is None:
            idx = len(self.q[eng])
            self._emit(eng, fn)
            self.last[eng] = (idx, fn, None, None)
            return None
        nm, n = inc
        sem = self.sem[nm]
        self.cnt[nm] += n
        val = self.cnt[nm]
        idx = len(self.q[eng])
        self._emit(eng, lambda e: fn(e).then_inc(sem, n))
        self.last[eng] = (idx, fn, nm, val)
        return val

    def wait(self, eng, nm, val=None):
        if isinstance(val, tuple):
            nm, val = val
        elif val is None:
            if nm is None or not isinstance(nm, tuple):
                return
            nm, val = nm
        if val is None or val <= 0:
            return
        sem = self.sem[nm]
        self._emit(eng, lambda e: e.wait_ge(sem, val))

    def mark(self, eng):
        nm = self.esem[eng]
        idx, fn, lsem, lval = self.last.get(eng, (None, None, None, None))
        if idx is None:
            return (nm, self.cnt[nm])
        if lsem is not None:
            return (lsem, lval)
        sem = self.sem[nm]
        self.cnt[nm] += 1
        val = self.cnt[nm]
        self.q[eng][idx] = (lambda e, f=fn, s=sem: f(e).then_inc(s, 1))
        self.last[eng] = (idx, fn, nm, val)
        return (nm, val)

    def barrier(self):
        toks = {e: self.mark(e) for e in ("tensor", "vector", "scalar", "gpsimd")}
        dl, ds, gd = self.cnt["DL"], self.cnt["DS"], self.cnt["GD"]
        for e in ENGINES:
            for e2 in ("tensor", "vector", "scalar", "gpsimd"):
                if e2 != e:
                    self.wait(e, toks[e2])
            self.wait(e, "DL", dl)
            self.wait(e, "DS", ds)
            self.wait(e, "GD", gd)
            self.dirty[e] = ([], [])

    def dma(self, out, in_, eng="sync", store=False):
        nm = "GD" if eng == "gpsimd" else ("DS" if store else "DL")
        self.wait(eng, nm, self.cnt[nm])
        val = self.op(eng, lambda e: e.dma_start(out=out, in_=in_), inc=(nm, 16), skip_track=True)
        return val

    def ts(self, out, in0, s1, s2=None, op0=ALU.mult, op1=ALU.bypass, inc=None):
        reads = [in0] + [a for a in (s1, s2) if hasattr(a, "tensor")]
        if s2 is None:
            return self.op("vector", lambda e: e.tensor_scalar(out=out, in0=in0, scalar1=s1, scalar2=None, op0=op0),
                           reads=reads, writes=[out], inc=inc)
        return self.op("vector", lambda e: e.tensor_scalar(out=out, in0=in0, scalar1=s1, scalar2=s2, op0=op0, op1=op1),
                       reads=reads, writes=[out], inc=inc)

    def tt(self, out, in0, in1, alu, inc=None):
        return self.op("vector", lambda e: e.tensor_tensor(out=out, in0=in0, in1=in1, op=alu),
                       reads=[in0, in1], writes=[out], inc=inc)

    def stt(self, out, in0, s, in1, op0, op1, accum=None, inc=None):
        reads = [in0, in1] + ([s] if hasattr(s, "tensor") else [])
        writes = [out] + ([accum] if accum is not None else [])
        return self.op("vector", lambda e: e.scalar_tensor_tensor(out=out, in0=in0, scalar=s, in1=in1,
                                                                  op0=op0, op1=op1, accum_out=accum),
                       reads=reads, writes=writes, inc=inc)

    def copy(self, out, in_, inc=None):
        return self.op("vector", lambda e: e.tensor_copy(out=out, in_=in_), reads=[in_], writes=[out], inc=inc)

    def red(self, out, in_, alu, absval=False, inc=None):
        return self.op("vector", lambda e: e.tensor_reduce(out=out, in_=in_, axis=AX, op=alu,
                                                           apply_absolute_value=absval),
                       reads=[in_], writes=[out], inc=inc)

    def recip(self, out, in_, inc=None):
        return self.op("vector", lambda e: e.reciprocal(out=out, in_=in_), reads=[in_], writes=[out], inc=inc)

    def act(self, out, in_, func, bias=0.0, scale=1.0, accum=None, inc=None):
        reads = [in_] + [a for a in (bias, scale) if hasattr(a, "tensor")]
        writes = [out] + ([accum] if accum is not None else [])
        return self.op("scalar", lambda e: e.activation(out=out, in_=in_, func=func, bias=bias, scale=scale,
                                                        accum_out=accum),
                       reads=reads, writes=writes, inc=inc)

    def mm(self, out, lhsT, rhs, start=True, stop=True, inc=None):
        return self.op("tensor", lambda e: e.matmul(out, lhsT, rhs, start=start, stop=stop),
                       inc=inc, skip_track=True)

    def tr(self, out, in_, ident, inc=None):
        return self.op("tensor", lambda e: e.transpose(out, in_, ident), inc=inc, skip_track=True)

    def memset(self, ap, val, inc=None):
        return self.op("gpsimd", lambda e: e.memset(ap, val), reads=[], writes=[ap], inc=inc)

    def affsel(self, ap, cmp, fill, pattern, chmul, inc=None):
        return self.op("gpsimd", lambda e: e.affine_select(out=ap, in_=ap, compare_op=cmp, fill=fill,
                                                           base=0, pattern=pattern, channel_multiplier=chmul),
                       reads=[ap], writes=[ap], inc=inc)


def build_program():
    nc = bass.Bass("TRN2", target_bir_lowering=False)
    es = contextlib.ExitStack()
    es.__enter__()

    dt_in = {}
    x0_d = nc.dram_tensor("L0_attn_x", [NSEQ, S, D], F32, kind="ExternalInput").ap()
    for name, shape in [
        ("L0_attn_Wq", [D, D]), ("L0_attn_Wk", [D, D]), ("L0_attn_Wv", [D, D]),
        ("L0_attn_Wo", [D, D]), ("L0_attn_rf", [HD, FD]),
        ("L0_ffn_w_gate", [INNER, D]), ("L0_ffn_w_up", [INNER, D]), ("L0_ffn_w_down", [D, INNER]),
        ("L1_attn_qkv_w", [192, D]), ("L1_attn_qkv_b", [192]),
        ("L1_attn_proj_w", [D, D]), ("L1_attn_proj_b", [D]),
        ("L1_ffn_w1", [INNER, D]), ("L1_ffn_b1", [INNER]),
        ("L1_ffn_w2", [D, INNER]), ("L1_ffn_b2", [D]),
    ]:
        dt_in[name] = nc.dram_tensor(name, shape, F32, kind="ExternalInput").ap()
    y_d = nc.dram_tensor("y", [NSEQ, S, D], F32, kind="ExternalOutput").ap()

    wgT_d = nc.dram_tensor("wgT_s", [NCH, 128, NCH, 4, 128], FP8).ap()
    wuT_d = nc.dram_tensor("wuT_s", [NCH, 128, NCH, 4, 128], FP8).ap()
    w1T_d = nc.dram_tensor("w1T_s", [NCH, 128, NCH, 4, 128], FP8).ap()
    wdT_d = nc.dram_tensor("wdT_s", [ICH, 128, NCH, 128], FP8).ap()
    w2T_d = nc.dram_tensor("w2T_s", [ICH, 128, NCH, 128], FP8).ap()
    wqrf_d = nc.dram_tensor("wqrf_s", [NCH, 128, H, 128], BF16).ap()
    wkrf_d = nc.dram_tensor("wkrf_s", [NCH, 128, H, 128], BF16).ap()
    if DEBUG_TAPS:
        dbg1_d = nc.dram_tensor("dbg_ffn0", [128, NCH * S], F32).ap()
        dbg2_d = nc.dram_tensor("dbg_x1", [128, NCH * S], F32).ap()
        dbg3_d = nc.dram_tensor("dbg_kvz", [128, H * 65], BF16).ap()
        dbg_xq_d = nc.dram_tensor("dbg_xq", [128, NCH * S], BF16).ap()
        dbg_h_d = nc.dram_tensor("dbg_h", [128, 2 * INNER], BF16).ap()
        dbg_tcf_d = nc.dram_tensor("dbg_tcf", [128, 16], F32).ap()
        dbg_hqT_d = nc.dram_tensor("dbg_hqT", [128, ICH * 256], BF16).ap()
        dbg5_d = nc.dram_tensor("dbg_xf1", [128, NCH * S], F32).ap()
        dbg_q_d = nc.dram_tensor("dbg_q", [64, S], BF16).ap()
        dbg_k_d = nc.dram_tensor("dbg_k", [64, S], BF16).ap()
        dbg_v_d = nc.dram_tensor("dbg_v", [128, NSUB * 64], BF16).ap()
        dbg_oh_d = nc.dram_tensor("dbg_oh", [64, S], BF16).ap()
        dbg_xn_d = nc.dram_tensor("dbg_xn", [128, NCH * S], BF16).ap()
        dbg_mr_d = nc.dram_tensor("dbg_mr", [2, S], F32).ap()
        dbg_bc_d = nc.dram_tensor("dbg_bc", [128, 1], F32).ap()
        dbg_P_d = nc.dram_tensor("dbg_P", [64, 1024], BF16).ap()

    x1acc = es.enter_context(nc.sbuf_tensor("x1acc", [128, NCH * S], F32))
    xnbf = es.enter_context(nc.sbuf_tensor("xnbf", [128, NCH * S], BF16))
    ident = es.enter_context(nc.sbuf_tensor("ident", [128, 128], BF16))
    identf = es.enter_context(nc.sbuf_tensor("identf", [128, 128], F32))
    trimask = es.enter_context(nc.sbuf_tensor("trimask", [128, 128], BF16))
    ones_r = es.enter_context(nc.sbuf_tensor("ones_r", [1, 128], BF16))
    ones_rf = es.enter_context(nc.sbuf_tensor("ones_rf", [1, 128], F32))
    ones_c = es.enter_context(nc.sbuf_tensor("ones_c", [128, 1], BF16))
    one11 = es.enter_context(nc.sbuf_tensor("one11", [1, 1], F32))
    cc = es.enter_context(nc.sbuf_tensor("cc", [128, 12], F32))
    CHALF, CM1, CPIH2, CPIHALL, CP1, CM2, CP2, CM3, CP3 = 0, 1, 2, 3, 4, 5, 6, 7, 8
    qkvT_sb = es.enter_context(nc.sbuf_tensor("qkvT_sb", [128, NCH * 192], BF16))
    PT_sb = es.enter_context(nc.sbuf_tensor("PT_sb", [64, D], BF16))
    rf_sb = es.enter_context(nc.sbuf_tensor("rf_sb", [64, FD], BF16))
    qkvb_c = es.enter_context(nc.sbuf_tensor("qkvb_c", [128, 1], F32))
    qkvb_vr = es.enter_context(nc.sbuf_tensor("qkvb_vr", [1, 64], BF16))
    projb_c = es.enter_context(nc.sbuf_tensor("projb_c", [128, NCH], F32))
    b1row = es.enter_context(nc.sbuf_tensor("b1row", [1, INNER], BF16))
    b2row = es.enter_context(nc.sbuf_tensor("b2row", [1, D], BF16))
    wmean2 = es.enter_context(nc.sbuf_tensor("wmean2", [128, 5], F32))
    MG, MU, MD, M1, M2 = range(5)
    invs_l0 = es.enter_context(nc.sbuf_tensor("invs_l0", [128, NSUB], F32))

    pb = PB(nc, es)

    def round_chain(w_ap, Ri_ap, d_ap, s1_ap, s2_ap, t12_ap, out_ap, wide=False):
        pb.copy(Ri_ap, w_ap)
        pb.tt(d_ap, w_ap, Ri_ap, ALU.subtract)
        js = {False: [-2, -1, 0, 1], "mid": [-3, -2, -1, 0, 1, 2, 3],
              True: [-3, -2, -1, 0, 1, 2, 3]}[wide]
        pb.ts(s1_ap, d_ap, float(js[0]), float(js[0] - 1), op0=ALU.is_gt, op1=ALU.add)
        for j in js[1:]:
            pb.stt(s1_ap, d_ap, float(j), s1_ap, ALU.is_gt, ALU.add)
        pb.tt(out_ap, s1_ap, Ri_ap, ALU.add)

    def wrap_sin(y_ap, n1_ap, out_ap):
        pb.ts(n1_ap, y_ap, PI, None, op0=ALU.is_gt)
        pb.stt(n1_ap, y_ap, -PI, n1_ap, ALU.is_gt, ALU.add)
        pb.ts(n1_ap, n1_ap, -1.0, None, op0=ALU.add)
        pb.stt(y_ap, n1_ap, -C2PI_HI, y_ap, ALU.mult, ALU.add)
        pb.stt(y_ap, n1_ap, -C2PI_LO, y_ap, ALU.mult, ALU.add)
        pb.ts(y_ap, y_ap, -PISAFE, PISAFE, op0=ALU.max, op1=ALU.min)
        v_w = pb.mark("vector")
        pb.wait("scalar", "VE", v_w)
        s_k = pb.act(out_ap, y_ap, AF.Sin, inc=("SE", 1))
        return s_k

    pb.memset(ident[:, :], 0.0)
    pb.affsel(ident[:, :], ALU.not_equal, 1.0, [[-1, 128]], 1)
    pb.memset(identf[:, :], 0.0)
    pb.affsel(identf[:, :], ALU.not_equal, 1.0, [[-1, 128]], 1)
    pb.memset(trimask[:, :], 0.0)
    pb.affsel(trimask[:, :], ALU.is_gt, 1.0, [[-1, 128]], 1)
    pb.memset(ones_r[:, :], 1.0)
    pb.memset(ones_rf[:, :], 1.0)
    pb.memset(ones_c[:, :], 1.0)
    pb.memset(one11[:, :], 1.0)
    pb.memset(cc[:, CHALF:CHALF + 1], 0.5)
    pb.memset(cc[:, CM1:CM1 + 1], -1.0)
    pb.memset(cc[0:64, CPIH2:CPIH2 + 1], 0.0)
    pb.memset(cc[64:128, CPIH2:CPIH2 + 1], 0.0)
    pb.memset(cc[:, CPIHALL:CPIHALL + 1], 0.0)
    pb.memset(cc[:, CP1:CP1 + 1], 1.0)
    pb.memset(cc[:, CM2:CM2 + 1], -2.0)
    pb.memset(cc[:, CP2:CP2 + 1], 2.0)
    pb.memset(cc[:, CM3:CM3 + 1], -3.0)
    pb.memset(cc[:, CP3:CP3 + 1], 3.0)
    pb.barrier()
    pb.ts(cc[0:64, CPIH2:CPIH2 + 1], cc[0:64, CPIH2:CPIH2 + 1], PIH64, None, op0=ALU.add)
    pb.ts(cc[:, CPIHALL:CPIHALL + 1], cc[:, CPIHALL:CPIHALL + 1], PIH64, None, op0=ALU.add)
    pb.barrier()

    with contextlib.ExitStack() as st_es:
        sb = nc.sbuf_tensor
        wt = st_es.enter_context(sb("st_wt", [128, INNER], F32))
        wt2 = st_es.enter_context(sb("st_wt2", [128, INNER], F32))
        s1t = st_es.enter_context(sb("st_s1", [128, INNER], BF16))
        s2t = st_es.enter_context(sb("st_s2", [128, INNER], BF16))
        ttile = st_es.enter_context(sb("st_ttile", [128, INNER], BF16))
        acc_c = st_es.enter_context(sb("st_acc", [128, ICH], F32))
        tiny = st_es.enter_context(sb("st_tiny", [128, 4], F32))
        rowt = st_es.enter_context(sb("st_rowt", [1, 128], F32))
        sc11 = st_es.enter_context(sb("st_sc11", [1, 4], F32))
        qkw = st_es.enter_context(sb("st_qkw", [128, D], BF16))
        qkw2 = st_es.enter_context(sb("st_qkw2", [64, D], BF16))
        pc64 = st_es.enter_context(sb("st_pc64", [128, 64], BF16))
        wkT = st_es.enter_context(sb("st_wkT", [64, 128], BF16))
        wrow = st_es.enter_context(sb("st_wrow", [128, FD], BF16))
        ps_b = st_es.enter_context(nc.psum_tensor("st_psb", [128, INNER], BF16))
        ps_f = st_es.enter_context(nc.psum_tensor("st_psf", [128, 256], F32))
        ps_g = st_es.enter_context(nc.psum_tensor("st_psg", [128, 128], BF16))
        ps_h = st_es.enter_context(nc.psum_tensor("st_psh", [128, 128], F32))
        ps_g2 = st_es.enter_context(nc.psum_tensor("st_psg2", [128, 128], F32))

        prow = wt[0:1, :]
        pwt = wt[:, 0:D]
        ld_rf = pb.dma(rf_sb[:, :], dt_in["L0_attn_rf"][:, :], eng="gpsimd")
        pb.dma(qkw[:, :], dt_in["L1_attn_qkv_w"][0:128, :], eng="gpsimd")
        ld_q2 = pb.dma(qkw2[:, :], dt_in["L1_attn_qkv_w"][128:192, :], eng="gpsimd")
        ldb = pb.dma(prow[0:1, 0:192], dt_in["L1_attn_qkv_b"][None, :])
        pb.wait("vector", "DL", ldb)
        pb.copy(qkvb_vr[0:1, :], prow[0:1, 128:192])
        pb.wait("tensor", "DL", ldb)
        t0 = pb.mm(ps_h[:, 0:1], prow[0:1, 0:128], one11[:, :], inc=("TE", 1))
        pb.wait("vector", "TE", t0)
        v0 = pb.copy(qkvb_c[:, 0:1], ps_h[:, 0:1], inc=("VE", 1))
        pb.wait("sync", "TE", pb.cnt["TE"])
        pb.wait("sync", pb.mark("vector"))
        ldb2 = pb.dma(prow[0:1, 0:D], dt_in["L1_attn_proj_b"][None, :])
        pb.wait("tensor", "DL", ldb2)
        pb.wait("tensor", "VE", v0)
        t1 = None
        for c in range(NCH):
            t1 = pb.mm(ps_f[:, c:c + 1], prow[0:1, 128 * c:128 * (c + 1)], one11[:, :], inc=("TE", 1))
        pb.wait("vector", "TE", t1)
        pb.copy(projb_c[:, :], ps_f[:, 0:NCH])
        pb.wait("sync", "TE", pb.cnt["TE"])
        pb.wait("sync", pb.mark("vector"))
        ldb3 = pb.dma(prow[0:1, 0:INNER], dt_in["L1_ffn_b1"][None, :])
        pb.wait("vector", "DL", ldb3)
        pb.copy(b1row[0:1, :], prow[0:1, :])
        pb.wait("sync", pb.mark("vector"))
        ldb4 = pb.dma(prow[0:1, 0:D], dt_in["L1_ffn_b2"][None, :])
        pb.wait("vector", "DL", ldb4)
        pb.copy(b2row[0:1, :], prow[0:1, 0:D])

        pb.wait("tensor", "GD", ld_q2)
        for i in range(NCH):
            pb.tr(ps_g[:, :], qkw[:, 128 * i:128 * (i + 1)], ident[:, :])
            t2 = pb.tr(ps_b[:, 0:64], qkw2[0:64, 128 * i:128 * (i + 1)], ident[0:64, 0:64], inc=("TE", 1))
            pb.wait("vector", "TE", t2)
            pb.copy(qkvT_sb[:, 192 * i:192 * i + 128], ps_g[:, :])
            v1 = pb.copy(qkvT_sb[:, 192 * i + 128:192 * (i + 1)], ps_b[:, 0:64], inc=("VE", 1))
            pb.wait("tensor", "VE", v1)

        pb.wait("sync", pb.mark("vector"))
        pb.wait("sync", "TE", pb.cnt["TE"])
        for c in range(NCH):
            ldp = pb.dma(pwt[:, :], dt_in["L1_attn_proj_w"][128 * c:128 * (c + 1), :])
            pb.wait("vector", "DL", ldp)
            pv = pwt[:, :].rearrange("p (a b) -> p b a", a=H)
            def _pred(e, pv=pv, pc=pc64):
                with nc.allow_low_precision(reason="P col-block sum written bf16 (weight operand)"):
                    return e.tensor_reduce(out=pc[:, :], in_=pv, axis=AX, op=ALU.add,
                                           apply_absolute_value=False)
            v2 = pb.op("vector", _pred, reads=[pv], writes=[pc64[:, :]], inc=("VE", 1))
            pb.wait("tensor", "VE", v2)
            t3 = pb.tr(ps_b[0:64, 0:128], pc64[:, :], ident[:, :], inc=("TE", 1))
            pb.wait("vector", "TE", t3)
            v3 = pb.copy(PT_sb[0:64, 128 * c:128 * (c + 1)], ps_b[0:64, 0:128], inc=("VE", 1))
            pb.wait("sync", "VE", v3)

        pb.wait("tensor", "GD", ld_rf)
        pb.wait("tensor", pb.mark("vector"))
        for (wname, wdst) in (("L0_attn_Wq", wqrf_d), ("L0_attn_Wk", wkrf_d)):
            for i in range(NCH):
                pb.wait("gpsimd", "GD", pb.cnt["GD"])
                pb.wait("gpsimd", "TE", pb.cnt["TE"])
                ldw = pb.dma(qkw[:, :], dt_in[wname][128 * i:128 * (i + 1), :], eng="gpsimd")
                pb.wait("tensor", "GD", ldw)
                for h in range(H):
                    t4 = pb.tr(ps_b[0:64, 0:128], qkw[:, 64 * h:64 * (h + 1)], ident[:, :], inc=("TE", 1))
                    pb.wait("vector", "TE", t4)
                    v4 = pb.copy(wkT[0:64, :], ps_b[0:64, 0:128], inc=("VE", 1))
                    pb.wait("tensor", "VE", v4)
                    t5 = pb.mm(ps_g2[:, :], wkT[0:64, :], rf_sb[0:64, :], inc=("TE", 1))
                    pb.wait("vector", "TE", t5)
                    v5 = pb.copy(wrow[:, :], ps_g2[:, :], inc=("VE", 1))
                    pb.wait("sync", "VE", v5)
                    st_w = pb.dma(wdst[i, :, h, :], wrow[:, :], store=True)
                    pb.wait("vector", "DS", st_w)
                pb.wait("gpsimd", "TE", pb.cnt["TE"])
        pb.barrier()

        ternspecs = [
            ("L0_ffn_w_gate", wgT_d, INNER, D, MG),
            ("L0_ffn_w_up", wuT_d, INNER, D, MU),
            ("L1_ffn_w1", w1T_d, INNER, D, M1),
            ("L0_ffn_w_down", wdT_d, D, INNER, MD),
            ("L1_ffn_w2", w2T_d, D, INNER, M2),
        ]
        for wname, wdst, NOUT, NIN, mcol in ternspecs:
            n_ot = NOUT // 128
            n_j = NIN // 128
            numel_inv = float(1.0 / (NOUT * NIN))
            pb.wait("scalar", "DS", pb.cnt["DS"])
            pb.wait("scalar", "VE", pb.cnt["VE"])
            pb.wait("scalar", "GD", pb.cnt["GD"])
            wts = (wt, wt2)
            lda = {}

            def ld_blk(i):
                lda[i] = pb.dma(wts[i % 2][:, 0:NIN], dt_in[wname][128 * i:128 * (i + 1), :])
            pb.wait("sync", "SE", pb.cnt["SE"])
            pb.wait("sync", "VE", pb.cnt["VE"])
            ld_blk(0)
            s_a = None
            for i in range(n_ot):
                pb.wait("scalar", "DL", lda[i])
                if i + 1 < n_ot:
                    if s_a is not None:
                        pb.wait("sync", "SE", s_a)
                    ld_blk(i + 1)
                s_a = pb.act(ttile[:, 0:NIN], wts[i % 2][:, 0:NIN], AF.Abs, accum=acc_c[:, i:i + 1],
                             inc=("SE", 1))
            pb.wait("vector", "SE", pb.cnt["SE"])
            pb.red(tiny[:, 2:3], acc_c[:, 0:n_ot], ALU.add)
            v6 = pb.mark("vector")
            pb.wait("tensor", "VE", v6)
            t6 = pb.tr(ps_h[0:1, 0:128], tiny[:, 2:3], identf[:, :], inc=("TE", 1))
            pb.wait("vector", "TE", t6)
            pb.copy(rowt[0:1, 0:128], ps_h[0:1, 0:128])
            pb.red(sc11[0:1, 0:1], rowt[0:1, 0:128], ALU.add)
            pb.ts(sc11[0:1, 1:2], sc11[0:1, 0:1], numel_inv, 1e-5, ALU.mult, ALU.max)
            pb.ts(sc11[0:1, 2:3], sc11[0:1, 1:2], 0.5, None, ALU.mult)
            v7 = pb.mark("vector")
            pb.wait("tensor", "VE", v7)
            t7 = pb.mm(ps_h[:, 0:1], ones_rf[0:1, :], sc11[0:1, 2:3], inc=("TE", 1))
            pb.wait("vector", "TE", t7)
            pb.copy(tiny[:, 0:1], ps_h[:, 0:1])
            pb.ts(tiny[:, 1:2], tiny[:, 0:1], -1.0, None, op0=ALU.mult)
            v8 = pb.copy(wmean2[:, mcol:mcol + 1], ps_h[:, 0:1], inc=("VE", 1))
            pb.wait("scalar", "VE", v8)
            pb.wait("tensor", "VE", v8)
            pb.wait("sync", "SE", pb.cnt["SE"])
            ld_blk(0)
            s_b = None
            t8s = {}
            sts = {}
            for i in range(n_ot):
                wti = wts[i % 2]
                q = (i % 4) * D if NIN == D else 0
                pb.wait("scalar", "DL", lda[i])
                if i + 1 < n_ot:
                    if s_b is not None:
                        pb.wait("sync", "SE", s_b)
                    ld_blk(i + 1)
                if NIN == D and i >= 4:
                    pb.wait("scalar", "TE", t8s[i - 4])
                elif NIN != D and i >= 1:
                    pb.wait("scalar", "TE", t8s[i - 1])
                pb.act(s1t[:, q:q + NIN], wti[:, 0:NIN], AF.Sign, bias=tiny[:, 1:2])
                s_b = pb.act(s2t[:, q:q + NIN], wti[:, 0:NIN], AF.Sign, bias=tiny[:, 0:1], inc=("SE", 1))
                pb.wait("vector", "SE", s_b)
                v9 = pb.tt(s1t[:, q:q + NIN], s1t[:, q:q + NIN], s2t[:, q:q + NIN], ALU.add, inc=("VE", 1))
                pb.wait("tensor", "VE", v9)
                t8 = None
                for j in range(n_j):
                    t8 = pb.tr(ps_b[:, q + 128 * j:q + 128 * (j + 1)],
                               s1t[:, q + 128 * j:q + 128 * (j + 1)], ident[:, :],
                               inc=("TE", 1) if j == n_j - 1 else None)
                t8s[i] = t8
                pb.wait("vector", "TE", t8)
                if NIN == D:
                    if i >= 4:
                        pb.wait("vector", "GD", sts[i - 4])
                    v10 = pb.copy(ttile[:, q:q + NIN], ps_b[:, q:q + NIN], inc=("VE", 1))
                    pb.wait("gpsimd", "VE", v10)
                    sts[i] = pb.dma(wdst[i // 4, :, :, i % 4, :],
                                    ttile[:, q:q + NIN].rearrange("p (j q2) -> p j q2", j=n_j),
                                    eng="gpsimd", store=True)
                else:
                    v10 = pb.copy(ttile[:, 0:NIN], ps_b[:, 0:NIN], inc=("VE", 1))
                    pb.wait("gpsimd", "VE", v10)
                    st_t = pb.dma(wdst[0:n_j, :, i, :].rearrange("j p q -> p j q"),
                                  ttile[:, 0:NIN].rearrange("p (j q) -> p j q", j=n_j),
                                  eng="gpsimd", store=True)
                    pb.wait("vector", "GD", st_t)
        pb.barrier()

    for sq in range(NSEQ):
        xs_d = x0_d[sq]
        ys_d = y_d[sq]
        tag = "_s%d" % sq
        sb = nc.sbuf_tensor

        xq_es = contextlib.ExitStack()
        xq_es.__enter__()
        xq1T = xq_es.enter_context(sb("xq1T" + tag, [128, NCH * S], BF16))
        with contextlib.ExitStack() as p0:
            x0t = p0.enter_context(sb("e_x0" + tag, [128, D], F32))
            junk = p0.enter_context(sb("e_junk" + tag, [128, D], BF16))
            tc = p0.enter_context(sb("e_tiny" + tag, [128, 12], F32))
            wtile = p0.enter_context(sb("e_w" + tag, [128, D], F32))
            Ri = p0.enter_context(sb("e_Ri" + tag, [128, D], I32))
            dtile = p0.enter_context(sb("e_d" + tag, [128, D], BF16))
            s1q = p0.enter_context(sb("e_s1" + tag, [128, D], BF16))
            s2q = p0.enter_context(sb("e_s2" + tag, [128, D], BF16))
            t12 = p0.enter_context(sb("e_t12" + tag, [128, D], BF16))
            xn_t = p0.enter_context(sb("e_xn" + tag, [128, D], BF16))
            xq_t = p0.enter_context(sb("e_xq" + tag, [128, D], BF16))
            psA = p0.enter_context(nc.psum_tensor("e_psA" + tag, [128, D], F32))
            psB = p0.enter_context(nc.psum_tensor("e_psB" + tag, [128, D], BF16))
            psC = p0.enter_context(nc.psum_tensor("e_psC" + tag, [128, D], BF16))

            for s in range(NSUB):
                ld = pb.dma(x0t[:, :], xs_d[128 * s:128 * (s + 1), :])
                pb.wait("vector", "DL", ld)
                pb.wait("scalar", "DL", ld)
                pb.red(tc[:, 9:10], x0t[:, :], ALU.add)
                s_sq = pb.act(junk[:, :], x0t[:, :], AF.Square, accum=tc[:, 10:11], inc=("SE", 1))
                pb.red(tc[:, 11:12], x0t[:, :], ALU.max)
                pb.red(tc[:, 6:7], x0t[:, :], ALU.min)
                pb.wait("vector", "SE", s_sq)
                pb.ts(tc[:, 0:1], tc[:, 9:10], 1.0 / 1024, None)
                pb.tt(tc[:, 2:3], tc[:, 0:1], tc[:, 0:1], ALU.mult)
                pb.stt(tc[:, 3:4], tc[:, 10:11], 1.0 / 1024, tc[:, 2:3], ALU.mult, ALU.subtract)
                pb.ts(tc[:, 3:4], tc[:, 3:4], 1e-5, None, op0=ALU.add)
                v_ve = pb.mark("vector")
                pb.wait("scalar", "VE", v_ve)
                pb.act(tc[:, 4:5], tc[:, 3:4], AF.Ln)
                s_r = pb.act(tc[:, 1:2], tc[:, 4:5], AF.Exp, scale=-0.5, inc=("SE", 1))
                pb.wait("vector", "SE", s_r)
                pb.tt(tc[:, 9:10], tc[:, 11:12], tc[:, 0:1], ALU.subtract)
                pb.tt(tc[:, 10:11], tc[:, 0:1], tc[:, 6:7], ALU.subtract)
                pb.tt(tc[:, 9:10], tc[:, 9:10], tc[:, 10:11], ALU.max)
                pb.tt(tc[:, 9:10], tc[:, 9:10], tc[:, 1:2], ALU.mult)
                pb.ts(tc[:, 9:10], tc[:, 9:10], 1e-5, None, op0=ALU.max)
                pb.recip(tc[:, 10:11], tc[:, 9:10])
                pb.ts(tc[:, 7:8], tc[:, 10:11], 127.0, None)
                pb.recip(tc[:, 8:9], tc[:, 7:8])
                pb.copy(invs_l0[:, s:s + 1], tc[:, 8:9])
                pb.tt(tc[:, 4:5], tc[:, 1:2], tc[:, 7:8], ALU.mult)
                pb.stt(tc[:, 5:6], tc[:, 0:1], -1.0, tc[:, 4:5], ALU.mult, ALU.mult)
                pb.ts(tc[:, 5:6], tc[:, 5:6], 0.5, None, op0=ALU.add)
                pb.stt(tc[:, 6:7], tc[:, 0:1], -1.0, tc[:, 1:2], ALU.mult, ALU.mult)
                v_st = pb.mark("vector")
                pb.wait("scalar", "VE", v_st)
                pb.act(xn_t[:, :], x0t[:, :], AF.Identity, scale=tc[:, 1:2], bias=tc[:, 6:7])
                s_w = pb.act(wtile[:, :], x0t[:, :], AF.Identity, scale=tc[:, 4:5], bias=tc[:, 5:6], inc=("SE", 1))
                pb.wait("vector", "SE", s_w)
                round_chain(wtile[:, :], Ri[:, :], dtile[:, :], s1q[:, :], s2q[:, :], t12[:, :], xq_t[:, :])
                v_xq = pb.mark("vector")
                pb.wait("tensor", "VE", v_xq)
                pb.wait("tensor", "SE", s_w)
                for c in range(NCH):
                    pb.tr(psA[:, 128 * c:128 * (c + 1)], x0t[:, 128 * c:128 * (c + 1)], identf[:, :])
                for c in range(NCH):
                    pb.tr(psB[:, 128 * c:128 * (c + 1)], xn_t[:, 128 * c:128 * (c + 1)], ident[:, :])
                t_tr = None
                for c in range(NCH):
                    t_tr = pb.tr(psC[:, 128 * c:128 * (c + 1)], xq_t[:, 128 * c:128 * (c + 1)], ident[:, :],
                                 inc=("TE", 1) if c == NCH - 1 else None)
                pb.wait("vector", "TE", t_tr)
                xa = x1acc[:, :].rearrange("p (c t) -> p c t", c=NCH)[:, :, 128 * s:128 * (s + 1)]
                pb.copy(xa, psA[:, :].rearrange("p (c t) -> p c t", c=NCH))
                xb_ = xnbf[:, :].rearrange("p (c t) -> p c t", c=NCH)[:, :, 128 * s:128 * (s + 1)]
                pb.copy(xb_, psB[:, :].rearrange("p (c t) -> p c t", c=NCH))
                xq1 = xq1T[:, :].rearrange("p (c t) -> p c t", c=NCH)[:, :, 128 * s:128 * (s + 1)]
                v_ev = pb.copy(xq1, psC[:, :].rearrange("p (c t) -> p c t", c=NCH), inc=("VE", 1))
                pb.wait("sync", "VE", v_ev)
                pb.wait("sync", "SE", pb.cnt["SE"])
                pb.wait("sync", "TE", t_tr)
                pb.wait("tensor", "VE", v_ev)
        pb.barrier()

        def emit_ffn(layer):
            ltag = tag + "_f%d" % layer
            with contextlib.ExitStack() as pf:
                wbufA = pf.enter_context(sb("f_wA" + ltag, [128, 2 * 4096], BF16))
                wbufD = wbufA
                h_sb = pf.enter_context(sb("f_h" + ltag, [128, 2 * INNER], BF16))
                hqT = pf.enter_context(sb("f_hqT" + ltag, [128, ICH * 256], BF16))
                wq5 = pf.enter_context(sb("f_w5" + ltag, [128, 512], BF16))
                Ri5 = pf.enter_context(sb("f_Ri5" + ltag, [128, 512], I32))
                d5 = pf.enter_context(sb("f_d5" + ltag, [128, 512], BF16))
                s15 = pf.enter_context(sb("f_s15" + ltag, [128, 512], BF16))
                s25 = wq5
                dsc = pf.enter_context(sb("f_dsc" + ltag, [128, 512], F32))
                sgt = dsc
                t125 = d5
                rq5 = s15
                junk5 = dsc
                tcf = pf.enter_context(sb("f_tc" + ltag, [128, 16], F32))
                hsq = pf.enter_context(sb("f_hsq" + ltag, [128, NCH], F32))
                rowb = pf.enter_context(sb("f_rowb" + ltag, [1, 256], BF16))
                xqL = pf.enter_context(sb("f_xqL" + ltag, [128, NCH * 256], BF16)) if layer == 1 else None
                cbt = dsc[:, 0:256] if layer == 1 else None
                bbt = dsc[:, 256:512] if layer == 1 else None

                gps = pf.enter_context(nc.psum_tensor("f_gps" + ltag, [128, 2 * 512], F32))
                gps2 = pf.enter_context(nc.psum_tensor("f_gps2" + ltag, [128, 2 * 512], F32))
                ups = pf.enter_context(nc.psum_tensor("f_ups" + ltag, [128, 2 * 512], F32)) \
                    if layer == 0 else gps2
                trp = pf.enter_context(nc.psum_tensor("f_trp" + ltag, [128, 1024], BF16))
                psT = pf.enter_context(nc.psum_tensor("f_psT" + ltag, [128, 512], F32))
                bcp = pf.enter_context(nc.psum_tensor("f_bcp" + ltag, [128, 512], F32)) \
                    if layer == 1 else None
                gpar = (gps, gps2)

                wA_d = wgT_d if layer == 0 else w1T_d
                wB_d = wuT_d if layer == 0 else None
                wD_d = wdT_d if layer == 0 else w2T_d
                mA = MG if layer == 0 else M1
                mD = MD if layer == 0 else M2

                for g in range(NCH):
                    subs = (2 * g, 2 * g + 1)
                    if layer == 0:
                        for k, s in enumerate(subs):
                            pb.tt(tcf[:, 4 * k + 0:4 * k + 1], invs_l0[:, s:s + 1], wmean2[:, MG:MG + 1], ALU.mult)
                            pb.tt(tcf[:, 4 * k + 1:4 * k + 2], invs_l0[:, s:s + 1], wmean2[:, MU:MU + 1], ALU.mult)
                    else:
                        t_bc = pb.mm(bcp[:, 0:256], ones_rf[0:1, :], c1row[0:1, 256 * g:256 * (g + 1)], inc=("TE", 1))
                        t_bc = pb.mm(bcp[:, 256:512], ones_rf[0:1, :], m1row[0:1, 256 * g:256 * (g + 1)], inc=("TE", 1))
                        pb.wait("vector", "TE", t_bc)
                        pb.copy(cbt[:, :], bcp[:, 0:256])
                        pb.copy(bbt[:, :], bcp[:, 256:512])
                        pb.stt(bbt[:, :], bbt[:, :], -1.0, cbt[:, :], ALU.mult, ALU.mult)
                        v_cb = pb.ts(bbt[:, :], bbt[:, :], 0.5, None, op0=ALU.add, inc=("VE", 1))
                        pb.wait("tensor", "VE", v_cb)
                        for c in range(NCH):
                            x1c = x1acc[:, S * c + 256 * g:S * c + 256 * (g + 1)]
                            pb.tt(wq5[:, 0:256], x1c, cbt[:, :], ALU.mult)
                            pb.tt(wq5[:, 0:256], wq5[:, 0:256], bbt[:, :], ALU.add)
                            round_chain(wq5[:, 0:256], Ri5[:, 0:256], d5[:, 0:256], s15[:, 0:256],
                                        s25[:, 0:256], t125[:, 0:256], xqL[:, 256 * c:256 * (c + 1)],
                                        wide="mid")
                        for k, s in enumerate(subs):
                            pb.tt(tcf[:, 4 * k + 0:4 * k + 1], s1col_all[:, s:s + 1], wmean2[:, M1:M1 + 1], ALU.mult)
                    v_sc = pb.mark("vector")
                    pb.wait("scalar", "VE", v_sc)
                    pb.wait("tensor", "VE", v_sc)

                    passes = (("g", wA_d), ("u", wB_d)) if layer == 0 else (("g", wA_d),)
                    steps = [(oc, pname, wdd) for oc in range(NCH) for pname, wdd in passes]
                    ldv = {}
                    t_of = {}
                    evac_gate = {}
                    v_h_last = None

                    def issue_load(si):
                        oc_, _pn, wdd_ = steps[si]
                        half = (si % 2) * 4096
                        ldv[si] = pb.dma(
                            wbufA[:, half:half + 4096].rearrange("p (j i q) -> p j i q", j=NCH, i=4),
                            wdd_[oc_, :, :, :, :], eng="gpsimd")
                    pb.wait("gpsimd", "TE", pb.cnt["TE"])
                    issue_load(0)
                    t_mm = None
                    for si, (oc, pname, wdd) in enumerate(steps):
                        half = (si % 2) * 4096
                        pb.wait("tensor", "GD", ldv[si])
                        if si + 1 < len(steps):
                            if si >= 1:
                                pb.wait("gpsimd", "TE", t_of[si - 1])
                            issue_load(si + 1)
                        if pname == "g":
                            dst = gpar[oc % 2]
                            if (oc % 2) in evac_gate:
                                pb.wait("tensor", "SE", evac_gate[oc % 2])
                        else:
                            dst = ups
                            if v_h_last is not None:
                                pb.wait("tensor", "VE", v_h_last)
                        for k, s in enumerate(subs):
                            for j in range(NCH):
                                if layer == 0:
                                    lt = xq1T[:, S * j + 128 * s:S * j + 128 * (s + 1)]
                                else:
                                    lt = xqL[:, 256 * j + 128 * k:256 * j + 128 * (k + 1)]
                                pb.mm(dst[:, 512 * k:512 * (k + 1)], lt,
                                      wbufA[:, half + 512 * j:half + 512 * (j + 1)],
                                      start=(j == 0), stop=(j == NCH - 1 and layer == 0))
                            if layer == 1:
                                pb.mm(dst[:, 512 * k:512 * (k + 1)], i1row[0:1, 128 * s:128 * (s + 1)],
                                      b1row[0:1, 512 * oc:512 * (oc + 1)], start=False, stop=True)
                        t_mm = t_of[si] = pb.mark("tensor")
                        if pname != passes[-1][0]:
                            continue
                        pb.wait("scalar", "TE", t_mm)
                        pb.wait("vector", "TE", t_mm)
                        v_h = None
                        for k, s in enumerate(subs):
                            hsl = h_sb[:, INNER * k + 512 * oc:INNER * k + 512 * (oc + 1)]
                            if layer == 0:
                                if v_h is not None:
                                    pb.wait("scalar", "VE", v_h)
                                s_g = pb.act(sgt[:, :], gpar[oc % 2][:, 512 * k:512 * (k + 1)], AF.Silu,
                                             scale=tcf[:, 4 * k:4 * k + 1], inc=("SE", 1))
                                pb.wait("vector", "SE", s_g)
                                v_h = pb.stt(hsl, ups[:, 512 * k:512 * (k + 1)], tcf[:, 4 * k + 1:4 * k + 2],
                                             sgt[:, :], ALU.mult, ALU.mult, inc=("VE", 1))
                            else:
                                v_h = pb.act(hsl, gpar[oc % 2][:, 512 * k:512 * (k + 1)], AF.Gelu,
                                             scale=tcf[:, 4 * k:4 * k + 1], inc=("SE", 1))
                        if layer == 0:
                            evac_gate[oc % 2] = s_g
                            v_h_last = v_h
                        else:
                            evac_gate[oc % 2] = v_h
                    if layer == 1:
                        pb.wait("vector", "SE", v_h)
                        pb.wait("tensor", "SE", v_h)
                    else:
                        pb.wait("tensor", "VE", v_h)

                    dld = {}

                    def issue_dload(jj):
                        dh = (jj % 2) * 1024
                        dld[jj] = pb.dma(wbufD[:, dh:dh + 1024],
                                         wD_d[jj, :, :, :].rearrange("p i q -> p (i q)"), eng="gpsimd")
                    pb.wait("gpsimd", "TE", pb.cnt["TE"])
                    issue_dload(0)

                    for k, s in enumerate(subs):
                        hrow = h_sb[:, INNER * k:INNER * (k + 1)]
                        pb.red(tcf[:, 4 * k + 2:4 * k + 3], hrow, ALU.max, absval=True)
                        if layer == 1:
                            for oc in range(NCH):
                                hsl = h_sb[:, INNER * k + 512 * oc:INNER * k + 512 * (oc + 1)]
                                pb.stt(junk5[:, :], hsl, 1.0, hsl, ALU.mult, ALU.mult, accum=hsq[:, oc:oc + 1])
                            pb.red(tcf[:, 4 * k + 3:4 * k + 4], hsq[:, :], ALU.add)
                            pb.ts(tcf[:, 4 * k + 3:4 * k + 4], tcf[:, 4 * k + 3:4 * k + 4], 1.0 / 4096, 1e-6,
                                  ALU.mult, ALU.add)
                            v_hs = pb.mark("vector")
                            pb.wait("scalar", "VE", v_hs)
                            pb.act(tcf[:, 4 * k + 3:4 * k + 4], tcf[:, 4 * k + 3:4 * k + 4], AF.Ln)
                            s_hs = pb.act(tcf[:, 4 * k + 3:4 * k + 4], tcf[:, 4 * k + 3:4 * k + 4], AF.Exp,
                                          scale=-0.5, inc=("SE", 1))
                            pb.wait("vector", "SE", s_hs)
                            pb.tt(tcf[:, 4 * k + 2:4 * k + 3], tcf[:, 4 * k + 2:4 * k + 3],
                                  tcf[:, 4 * k + 3:4 * k + 4], ALU.mult)
                        pb.ts(tcf[:, 4 * k + 2:4 * k + 3], tcf[:, 4 * k + 2:4 * k + 3], 1e-5, None, op0=ALU.max)
                        pb.recip(tcf[:, 8 + k:9 + k], tcf[:, 4 * k + 2:4 * k + 3])
                        pb.ts(tcf[:, 8 + k:9 + k], tcf[:, 8 + k:9 + k], 127.0, None)
                        pb.recip(tcf[:, 10 + k:11 + k], tcf[:, 8 + k:9 + k])
                        if layer == 1:
                            pb.tt(tcf[:, 8 + k:9 + k], tcf[:, 8 + k:9 + k], tcf[:, 4 * k + 3:4 * k + 4], ALU.mult)
                        pb.tt(tcf[:, 12 + k:13 + k], tcf[:, 10 + k:11 + k], wmean2[:, mD:mD + 1], ALU.mult)
                    v_hsc = pb.mark("vector")
                    pb.wait("scalar", "VE", v_hsc)

                    for k, s in enumerate(subs):
                        for oc4 in range(NCH):
                            hsl = h_sb[:, INNER * k + 512 * oc4:INNER * k + 512 * (oc4 + 1)]
                            s_w5 = pb.act(wq5[:, :], hsl, AF.Identity, scale=tcf[:, 8 + k:9 + k],
                                          bias=cc[:, CHALF:CHALF + 1], inc=("SE", 1))
                            pb.wait("vector", "SE", s_w5)
                            round_chain(wq5[:, :], Ri5[:, :], d5[:, :], s15[:, :], s25[:, :], t125[:, :], rq5[:, :],
                                        wide=True)
                            v_rq = pb.mark("vector")
                            pb.wait("tensor", "VE", v_rq)
                            tb = 512 * (oc4 % 2)
                            t_tq = None
                            for b in range(4):
                                t_tq = pb.tr(trp[:, tb + 128 * b:tb + 128 * (b + 1)],
                                             rq5[:, 128 * b:128 * (b + 1)], ident[:, :],
                                             inc=("TE", 1) if b == 3 else None)
                            pb.wait("vector", "TE", t_tq)
                            jj0 = 4 * oc4
                            dstv = hqT[:, 256 * jj0:256 * (jj0 + 4)].rearrange(
                                "p (j t2) -> p j t2", j=4)[:, :, 128 * k:128 * (k + 1)]
                            v_hq = pb.copy(dstv, trp[:, tb:tb + 512].rearrange("p (j t) -> p j t", j=4),
                                           inc=("VE", 1))
                            pb.wait("tensor", "VE", v_hq)
                            pb.wait("scalar", "VE", v_hq)

                    if layer == 1:
                        for k, s in enumerate(subs):
                            pb.recip(tcf[:, 14 + k:15 + k], tcf[:, 12 + k:13 + k])
                            v_i2 = pb.mark("vector")
                            pb.wait("tensor", "VE", v_i2)
                            t_i2 = pb.tr(psT[0:1, 0:128], tcf[:, 14 + k:15 + k], identf[:, :], inc=("TE", 1))
                            pb.wait("vector", "TE", t_i2)
                            pb.copy(rowb[0:1, 128 * k:128 * (k + 1)], psT[0:1, 0:128])
                        v_i2b = pb.mark("vector")
                        pb.wait("tensor", "VE", v_i2b)

                    if DEBUG_TAPS and sq == 0 and layer == 0 and g == DEBUG_TAP_GROUP:
                        pb.barrier()
                        pb.dma(dbg_h_d[:, :], h_sb[:, :], store=True)
                        pb.barrier()
                        pb.dma(dbg_tcf_d[:, 0:3], tcf[:, 0:3], store=True)
                        pb.barrier()
                        pb.dma(dbg_tcf_d[:, 4:7], tcf[:, 4:7], store=True)
                        pb.barrier()
                        pb.dma(dbg_tcf_d[:, 8:14], tcf[:, 8:14], store=True)
                        pb.barrier()
                        pb.dma(dbg_hqT_d[:, :], hqT[:, :], store=True)
                        pb.barrier()

                    def dreg(k, half):
                        t = gps if k == 0 else ups
                        return t[:, 512 * half:512 * (half + 1)]
                    t_dof = {}
                    for jj in range(ICH):
                        dhalf = (jj % 2) * 1024
                        pb.wait("tensor", "GD", dld[jj])
                        if jj + 1 < ICH:
                            if jj >= 1:
                                pb.wait("gpsimd", "TE", t_dof[jj - 1])
                            issue_dload(jj + 1)
                        for k in range(2):
                            for half in range(2):
                                pb.mm(dreg(k, half), hqT[:, 256 * jj + 128 * k:256 * jj + 128 * (k + 1)],
                                      wbufD[:, dhalf + 512 * half:dhalf + 512 * (half + 1)],
                                      start=(jj == 0), stop=(jj == ICH - 1 and layer == 0))
                        t_dof[jj] = pb.mark("tensor")
                    if layer == 1:
                        for k, s in enumerate(subs):
                            for half in range(2):
                                pb.mm(dreg(k, half), rowb[0:1, 128 * k:128 * (k + 1)],
                                      b2row[0:1, 512 * half:512 * (half + 1)], start=False, stop=True)
                    t_dn = pb.mark("tensor")
                    pb.wait("scalar", "TE", t_dn)
                    for k, s in enumerate(subs):
                        for half in range(2):
                            s_dc = pb.act(dsc[:, :], dreg(k, half), AF.Identity, scale=tcf[:, 12 + k:13 + k],
                                          inc=("SE", 1))
                            pb.wait("tensor", "SE", s_dc)
                            t_dt = None
                            for b in range(4):
                                t_dt = pb.tr(psT[:, 128 * b:128 * (b + 1)], dsc[:, 128 * b:128 * (b + 1)],
                                             identf[:, :], inc=("TE", 1) if b == 3 else None)
                            pb.wait("vector", "TE", t_dt)
                            xsl = x1acc[:, :].rearrange("p (c t) -> p c t", c=NCH)[
                                :, 4 * half:4 * (half + 1), 128 * s:128 * (s + 1)]
                            v_dn = pb.stt(xsl, psT[:, :].rearrange("p (b t) -> p b t", b=4), 1.0, xsl,
                                          ALU.mult, ALU.add, inc=("VE", 1))
                            pb.wait("tensor", "VE", v_dn)
                            pb.wait("scalar", "VE", v_dn)

        if DEBUG_TAPS and sq == 0:
            pb.dma(dbg_xq_d[:, :], xq1T[:, :], store=True)
            pb.barrier()
        emit_ffn(0)
        pb.barrier()
        if DEBUG_TAPS and sq == 0:
            pb.dma(dbg1_d[:, :], x1acc[:, :], store=True)
            pb.barrier()
        xq_es.__exit__(None, None, None)
        kvz_es = contextlib.ExitStack()
        kvz_es.__enter__()
        kvz_sb = kvz_es.enter_context(sb("kvz" + tag, [128, H * 65], BF16))

        with contextlib.ExitStack() as p2:
            wkrf_sb = p2.enter_context(sb("a_wkrf" + tag, [128, NCH * 2048], BF16))
            wv_sb = p2.enter_context(sb("a_wv" + tag, [128, NCH * 1024], BF16))
            kp_sb = p2.enter_context(sb("a_kp" + tag, [128, 2048], BF16))
            vv_sb = p2.enter_context(sb("a_vv" + tag, [128, H * 65], BF16))
            ytile = p2.enter_context(sb("a_y" + tag, [128, 512], F32))
            n1t = p2.enter_context(sb("a_n1" + tag, [128, 512], BF16))
            kvzp = p2.enter_context(nc.psum_tensor("a_kvzp" + tag, [128, H * 128], F32))
            pp1 = p2.enter_context(nc.psum_tensor("a_pp1" + tag, [128, 512], F32))
            pp2 = p2.enter_context(nc.psum_tensor("a_pp2" + tag, [128, 512], F32))

            pb.dma(wkrf_sb[:, :].rearrange("p (i h q) -> p i h q", i=NCH, h=H),
                   wkrf_d[0:NCH, :, :, :].rearrange("i p h q -> p i h q"))
            ld_wv = pb.dma(wv_sb[:, :].rearrange("p (i q) -> p i q", i=NCH),
                           dt_in["L0_attn_Wv"][:, :].rearrange("(i p) q -> p i q", p=128), eng="gpsimd")
            pb.wait("gpsimd", "GD", ld_wv)
            pb.memset(vv_sb[:, :].rearrange("p (h q) -> p h q", h=H)[:, :, 64:65], 1.0)
            g_on = pb.mark("gpsimd")
            pb.wait("tensor", "GE", g_on)
            pb.wait("tensor", "DL", pb.cnt["DL"])
            pb.wait("tensor", "GD", pb.cnt["GD"])

            for s in range(NSUB):
                t_v = None
                for half in range(2):
                    pp = pp1 if half == 0 else pp2
                    for j in range(NCH):
                        t_v = pb.mm(pp[:, :], xnbf[:, S * j + 128 * s:S * j + 128 * (s + 1)],
                                    wv_sb[:, 1024 * j + 512 * half:1024 * j + 512 * (half + 1)],
                                    start=(j == 0), stop=(j == NCH - 1),
                                    inc=("TE", 1) if (half == 1 and j == NCH - 1) else None)
                pb.wait("vector", "TE", t_v)
                vvv = vv_sb[:, :].rearrange("p (h q) -> p h q", h=H)[:, :, 0:64]
                pb.copy(vvv[:, 0:8, :], pp1[:, :].rearrange("p (h q) -> p h q", h=8))
                v_v = pb.copy(vvv[:, 8:16, :], pp2[:, :].rearrange("p (h q) -> p h q", h=8), inc=("VE", 1))
                pb.wait("tensor", "VE", v_v)
                for fs in range(4):
                    pp = pp1 if fs % 2 == 0 else pp2
                    t_k = None
                    for j in range(NCH):
                        t_k = pb.mm(pp[:, :], xnbf[:, S * j + 128 * s:S * j + 128 * (s + 1)],
                                    wkrf_sb[:, 2048 * j + 512 * fs:2048 * j + 512 * (fs + 1)],
                                    start=(j == 0), stop=(j == NCH - 1),
                                    inc=("TE", 1) if j == NCH - 1 else None)
                    pb.wait("scalar", "TE", t_k)
                    ppv = pp[:, :].rearrange("p (h q) -> p h q", h=4)
                    ytv = ytile[:, :].rearrange("p (h q) -> p h q", h=4)
                    pb.act(ytv[:, :, 0:64], ppv[:, :, 0:64], AF.Identity, bias=cc[:, CPIHALL:CPIHALL + 1])
                    s_y = pb.act(ytv[:, :, 64:128], ppv[:, :, 64:128], AF.Identity, inc=("SE", 1))
                    pb.wait("vector", "SE", s_y)
                    s_k = wrap_sin(ytile[:, :], n1t[:, :], kp_sb[:, 512 * fs:512 * (fs + 1)])
                    pb.wait("tensor", "SE", s_k)
                t_kv = None
                for h in range(H):
                    t_kv = pb.mm(kvzp[:, 128 * h:128 * h + 65], kp_sb[:, 128 * h:128 * (h + 1)],
                                 vv_sb[:, 65 * h:65 * (h + 1)],
                                 start=(s == 0 and h % 4 == 0), stop=(s == NSUB - 1 and h % 4 == 3),
                                 inc=("TE", 1) if h == H - 1 else None)
                pb.wait("vector", "TE", t_kv)
                pb.wait("scalar", "TE", t_kv)
            kvv = kvzp[:, :].rearrange("p (h q) -> p h q", h=H)[:, :, 0:65]
            v_kvz = pb.copy(kvz_sb[:, :].rearrange("p (h q) -> p h q", h=H), kvv, inc=("VE", 1))
            pb.wait("tensor", "VE", v_kvz)
        pb.barrier()

        with contextlib.ExitStack() as p4:
            wqrf_sb = p4.enter_context(sb("b_wqrf" + tag, [128, NCH * 2048], BF16))
            wo_sb = p4.enter_context(sb("b_wo" + tag, [128, NCH * 1024], BF16))
            qpT = p4.enter_context(sb("b_qp" + tag, [128, 512], BF16))
            ho_sb = p4.enter_context(sb("b_ho" + tag, [128, 4 * 1024], BF16))
            hoT = p4.enter_context(sb("b_hoT" + tag, [128, NCH * 512], BF16))
            ytile = p4.enter_context(sb("b_y" + tag, [128, 512], F32))
            n1t = p4.enter_context(sb("b_n1" + tag, [128, 512], BF16))
            zcol = p4.enter_context(sb("b_z" + tag, [128, 4], F32))
            qq = p4.enter_context(nc.psum_tensor("b_qq" + tag, [128, 512], F32))
            qq2 = p4.enter_context(nc.psum_tensor("b_qq2" + tag, [128, 512], F32))
            zp = p4.enter_context(nc.psum_tensor("b_zp" + tag, [128, 4 * 128], F32))
            trb = p4.enter_context(nc.psum_tensor("b_trb" + tag, [128, 1024], BF16))
            wop = p4.enter_context(nc.psum_tensor("b_wop" + tag, [128, 512], F32))

            pb.dma(wqrf_sb[:, :].rearrange("p (i h q) -> p i h q", i=NCH, h=H),
                   wqrf_d[0:NCH, :, :, :].rearrange("i p h q -> p i h q"))
            pb.dma(wo_sb[:, :].rearrange("p (i q) -> p i q", i=NCH),
                   dt_in["L0_attn_Wo"][:, :].rearrange("(i p) q -> p i q", p=128), eng="gpsimd")
            pb.wait("tensor", "DL", pb.cnt["DL"])
            pb.wait("tensor", "GD", pb.cnt["GD"])

            for g in range(4):
                for h in range(H):
                    pp = qq if h % 2 == 0 else qq2
                    t_q = None
                    for j in range(NCH):
                        t_q = pb.mm(pp[:, :], wqrf_sb[:, 2048 * j + 128 * h:2048 * j + 128 * (h + 1)],
                                    xnbf[:, S * j + 512 * g:S * j + 512 * (g + 1)],
                                    start=(j == 0), stop=(j == NCH - 1),
                                    inc=("TE", 1) if j == NCH - 1 else None)
                    pb.wait("scalar", "TE", t_q)
                    s_y = pb.act(ytile[:, :], pp[:, :], AF.Identity, bias=cc[:, CPIH2:CPIH2 + 1], inc=("SE", 1))
                    pb.wait("vector", "SE", s_y)
                    s_k = wrap_sin(ytile[:, :], n1t[:, :], qpT[:, :])
                    pb.wait("tensor", "SE", s_k)
                    t_z = None
                    for st in range(4):
                        t_z = pb.mm(zp[:, 128 * st:128 * st + 65], qpT[:, 128 * st:128 * (st + 1)],
                                    kvz_sb[:, 65 * h:65 * (h + 1)],
                                    inc=("TE", 1) if st == 3 else None)
                    pb.wait("vector", "TE", t_z)
                    for st in range(4):
                        pb.ts(zcol[:, st:st + 1], zp[:, 128 * st + 64:128 * st + 65], ZCLIP, None, op0=ALU.max)
                    v_z = pb.mark("vector")
                    pb.recip(zcol[:, 0:4], zcol[:, 0:4])
                    v_z = pb.mark("vector")
                    pb.wait("scalar", "VE", v_z)
                    s_ho = None
                    for st in range(4):
                        s_ho = pb.act(ho_sb[:, 1024 * st + 64 * h:1024 * st + 64 * (h + 1)],
                                      zp[:, 128 * st:128 * st + 64], AF.Identity, scale=zcol[:, st:st + 1],
                                      inc=("SE", 1) if st == 3 else None)
                    pb.wait("tensor", "SE", s_ho)
                for st in range(4):
                    t_h = None
                    for c in range(NCH):
                        t_h = pb.tr(trb[:, 128 * c:128 * (c + 1)],
                                    ho_sb[:, 1024 * st + 128 * c:1024 * st + 128 * (c + 1)], ident[:, :],
                                    inc=("TE", 1) if c == NCH - 1 else None)
                    pb.wait("vector", "TE", t_h)
                    hv = hoT[:, :].rearrange("p (c t) -> p c t", c=NCH)[:, :, 128 * st:128 * (st + 1)]
                    v_h2 = pb.copy(hv, trb[:, :].rearrange("p (c t) -> p c t", c=NCH), inc=("VE", 1))
                    pb.wait("tensor", "VE", v_h2)
                for oc in range(NCH):
                    t_w = None
                    for j in range(NCH):
                        t_w = pb.mm(wop[:, :], wo_sb[:, 1024 * j + 128 * oc:1024 * j + 128 * (oc + 1)],
                                    hoT[:, 512 * j:512 * (j + 1)],
                                    start=(j == 0), stop=(j == NCH - 1),
                                    inc=("TE", 1) if j == NCH - 1 else None)
                    pb.wait("vector", "TE", t_w)
                    xsl = x1acc[:, S * oc + 512 * g:S * oc + 512 * (g + 1)]
                    v_w = pb.stt(xsl, wop[:, :], 1.0, xsl, ALU.mult, ALU.add, inc=("VE", 1))
                    pb.wait("tensor", "VE", v_w)
        pb.barrier()

        if DEBUG_TAPS and sq == 0:
            pb.dma(dbg2_d[:, :], x1acc[:, :], store=True)
            pb.dma(dbg3_d[:, :], kvz_sb[:, :], store=True)
            pb.barrier()
        kvz_es.__exit__(None, None, None)

        prow_es = contextlib.ExitStack()
        prow_es.__enter__()
        m1row = prow_es.enter_context(sb("m1row" + tag, [1, S], F32))
        c1row = prow_es.enter_context(sb("c1row" + tag, [1, S], F32))
        i1row = prow_es.enter_context(sb("i1row" + tag, [1, S], BF16))
        s1col_all = prow_es.enter_context(sb("s1col" + tag, [128, NSUB], F32))
        attn1_es = contextlib.ExitStack()
        attn1_es.__enter__()
        qT_t = attn1_es.enter_context(sb("qT_t" + tag, [64, S], BF16))
        kT_t = attn1_es.enter_context(sb("kT_t" + tag, [64, S], BF16))
        v1_sb = attn1_es.enter_context(sb("v1_sb" + tag, [128, NSUB * 64], BF16))
        ohT = attn1_es.enter_context(sb("ohT" + tag, [64, S], BF16))
        r1_es = contextlib.ExitStack()
        r1_es.__enter__()
        r1row = r1_es.enter_context(sb("r1row" + tag, [1, S], F32))
        with contextlib.ExitStack() as p5:
            sqt = p5.enter_context(sb("p5_sq" + tag, [128, 512], BF16))
            runmx = p5.enter_context(sb("p5_mx" + tag, [128, S], F32))
            runmn = p5.enter_context(sb("p5_mn" + tag, [128, S], F32))
            colsc = p5.enter_context(sb("p5_col" + tag, [128, 4], F32))
            rowA = p5.enter_context(sb("p5_ra" + tag, [1, S], F32))
            rowB = p5.enter_context(sb("p5_rb" + tag, [1, S], F32))
            mp = p5.enter_context(nc.psum_tensor("p5_mp" + tag, [1, 512], F32))
            qp = p5.enter_context(nc.psum_tensor("p5_qp" + tag, [1, 512], F32))
            tp = p5.enter_context(nc.psum_tensor("p5_tp" + tag, [128, 256], F32))

            v_cast = None
            for c in range(NCH):
                v_cast = pb.copy(xnbf[:, S * c:S * (c + 1)], x1acc[:, S * c:S * (c + 1)],
                                 inc=("VE", 1) if c == NCH - 1 else None)
            pb.wait("tensor", "VE", v_cast)
            pb.wait("scalar", "VE", v_cast)
            for tsl in range(4):
                t_m = None
                for c in range(NCH):
                    sl = xnbf[:, S * c + 512 * tsl:S * c + 512 * (tsl + 1)]
                    s_sq = pb.act(sqt[:, :], sl, AF.Square, inc=("SE", 1))
                    pb.mm(mp[:, :], ones_c[:, :], sl, start=(c == 0), stop=(c == NCH - 1))
                    pb.wait("tensor", "SE", s_sq)
                    t_m = pb.mm(qp[:, :], ones_c[:, :], sqt[:, :], start=(c == 0), stop=(c == NCH - 1),
                                inc=("TE", 1))
                    pb.wait("scalar", "TE", t_m)
                pb.wait("scalar", "TE", t_m)
                s_ev = pb.act(m1row[0:1, 512 * tsl:512 * (tsl + 1)], mp[:, :], AF.Identity, scale=1.0 / 1024)
                s_ev = pb.act(rowA[0:1, 512 * tsl:512 * (tsl + 1)], qp[:, :], AF.Identity, scale=1.0 / 1024,
                              inc=("SE", 1))
                pb.wait("tensor", "SE", s_ev)
                pb.wait("vector", "SE", s_ev)
            pb.tt(rowB[0:1, :], m1row[0:1, :], m1row[0:1, :], ALU.mult)
            pb.tt(rowA[0:1, :], rowA[0:1, :], rowB[0:1, :], ALU.subtract)
            pb.ts(rowB[0:1, :], rowA[0:1, :], 1e-5, None, op0=ALU.add)
            v_rows = pb.mark("vector")
            pb.wait("scalar", "VE", v_rows)
            pb.act(r1row[0:1, :], rowB[0:1, :], AF.Ln)
            s_rw = pb.act(r1row[0:1, :], r1row[0:1, :], AF.Exp, scale=-0.5, inc=("SE", 1))
            pb.wait("vector", "SE", s_rw)
            pb.recip(rowB[0:1, :], rowB[0:1, :])
            pb.tt(rowA[0:1, :], rowA[0:1, :], rowB[0:1, :], ALU.mult)
            pb.ts(rowA[0:1, :], rowA[0:1, :], 1e-6, None, op0=ALU.add)
            v_rw2 = pb.mark("vector")
            pb.wait("scalar", "VE", v_rw2)
            pb.act(rowA[0:1, :], rowA[0:1, :], AF.Ln)
            s_rw2 = pb.act(rowA[0:1, :], rowA[0:1, :], AF.Exp, scale=-0.5, inc=("SE", 1))
            pb.wait("vector", "SE", s_rw2)
            pb.copy(runmx[:, :], x1acc[:, 0:S])
            pb.copy(runmn[:, :], x1acc[:, 0:S])
            for c in range(1, NCH):
                pb.tt(runmx[:, :], runmx[:, :], x1acc[:, S * c:S * (c + 1)], ALU.max)
                pb.tt(runmn[:, :], runmn[:, :], x1acc[:, S * c:S * (c + 1)], ALU.min)
            v_tree = pb.mark("vector")
            pb.wait("tensor", "VE", v_tree)
            for s in range(NSUB):
                pb.tr(tp[:, 0:128], runmx[:, 128 * s:128 * (s + 1)], identf[:, :])
                t_t = pb.tr(tp[:, 128:256], runmn[:, 128 * s:128 * (s + 1)], identf[:, :], inc=("TE", 1))
                pb.wait("vector", "TE", t_t)
                pb.red(colsc[:, 0:1], tp[:, 0:128], ALU.max)
                v_c0 = pb.red(colsc[:, 1:2], tp[:, 128:256], ALU.min, inc=("VE", 1))
                pb.wait("tensor", "VE", v_c0)
                t_mc = pb.tr(tp[:, 0:1], m1row[0:1, 128 * s:128 * (s + 1)], identf[0:1, 0:1], inc=("TE", 1))
                pb.wait("vector", "TE", t_mc)
                pb.tt(colsc[:, 0:1], colsc[:, 0:1], tp[:, 0:1], ALU.subtract)
                pb.tt(colsc[:, 1:2], tp[:, 0:1], colsc[:, 1:2], ALU.subtract)
                v_c1 = pb.tt(colsc[:, 2:3], colsc[:, 0:1], colsc[:, 1:2], ALU.max, inc=("VE", 1))
                pb.wait("tensor", "VE", v_c1)
                t_mx = pb.tr(tp[0:1, 0:128], colsc[:, 2:3], identf[:, :], inc=("TE", 1))
                pb.wait("vector", "TE", t_mx)
                v_mx = pb.copy(rowB[0:1, 128 * s:128 * (s + 1)], tp[0:1, 0:128], inc=("VE", 1))
                pb.wait("tensor", "VE", v_mx)
            pb.tt(rowB[0:1, :], rowB[0:1, :], r1row[0:1, :], ALU.mult)
            pb.tt(rowB[0:1, :], rowB[0:1, :], rowA[0:1, :], ALU.mult)
            pb.ts(rowB[0:1, :], rowB[0:1, :], 1e-5, None, op0=ALU.max)
            pb.recip(rowB[0:1, :], rowB[0:1, :])
            pb.ts(rowB[0:1, :], rowB[0:1, :], 127.0, None)
            pb.tt(c1row[0:1, :], r1row[0:1, :], rowA[0:1, :], ALU.mult)
            pb.tt(c1row[0:1, :], c1row[0:1, :], rowB[0:1, :], ALU.mult)
            pb.recip(rowA[0:1, 0:1], wmean2[0:1, M1:M1 + 1])
            pb.op("vector", lambda e, _o=i1row[0:1, :], _i=rowB[0:1, :], _s=rowA[0:1, 0:1]:
                  e.tensor_single_scalar(out=_o, in_=_i, scalar=_s, op=ALU.mult),
                  reads=[rowB[0:1, :], rowA[0:1, 0:1]], writes=[i1row[0:1, :]])
            pb.recip(rowA[0:1, :], rowB[0:1, :])
            v_inv = pb.mark("vector")
            pb.wait("tensor", "VE", v_inv)
            for s in range(NSUB):
                t_iv = pb.tr(tp[:, 0:1], rowA[0:1, 128 * s:128 * (s + 1)], identf[0:1, 0:1], inc=("TE", 1))
                pb.wait("vector", "TE", t_iv)
                v_iv = pb.copy(s1col_all[:, s:s + 1], tp[:, 0:1], inc=("VE", 1))
                pb.wait("tensor", "VE", v_iv)
        pb.barrier()

        with contextlib.ExitStack() as p6:
            mbt = p6.enter_context(sb("p6_mb" + tag, [128, 512], F32))
            rbt = p6.enter_context(sb("p6_rb" + tag, [128, 512], F32))
            scr = p6.enter_context(sb("p6_scr" + tag, [128, 512], F32))
            bp = p6.enter_context(nc.psum_tensor("p6_bp" + tag, [128, 1024], F32))
            qkp = p6.enter_context(nc.psum_tensor("p6_qkp" + tag, [128, 512], F32))
            vp = p6.enter_context(nc.psum_tensor("p6_vp" + tag, [128, 128], F32))

            for g in range(4):
                t_b = pb.mm(bp[:, 0:512], ones_rf[0:1, :], m1row[0:1, 512 * g:512 * (g + 1)], inc=("TE", 1))
                t_b = pb.mm(bp[:, 512:1024], ones_rf[0:1, :], r1row[0:1, 512 * g:512 * (g + 1)], inc=("TE", 1))
                pb.wait("vector", "TE", t_b)
                pb.copy(mbt[:, :], bp[:, 0:512])
                v_b = pb.copy(rbt[:, :], bp[:, 512:1024], inc=("VE", 1))
                pb.wait("tensor", "VE", v_b)
                v_xn = None
                for c in range(NCH):
                    xsl = x1acc[:, S * c + 512 * g:S * c + 512 * (g + 1)]
                    pb.tt(scr[:, :], xsl, mbt[:, :], ALU.subtract)
                    v_xn = pb.tt(xnbf[:, S * c + 512 * g:S * c + 512 * (g + 1)], scr[:, :], rbt[:, :], ALU.mult,
                                 inc=("VE", 1) if c == NCH - 1 else None)
                pb.wait("tensor", "VE", v_xn)
                t_qk = None
                for j in range(NCH):
                    t_qk = pb.mm(qkp[:, :], qkvT_sb[:, 192 * j:192 * j + 128],
                                 xnbf[:, S * j + 512 * g:S * j + 512 * (g + 1)],
                                 start=(j == 0), stop=(j == NCH - 1), inc=("TE", 1) if j == NCH - 1 else None)
                pb.wait("scalar", "TE", t_qk)
                pb.act(qT_t[0:64, 512 * g:512 * (g + 1)], qkp[0:64, :], AF.Identity, bias=qkvb_c[0:64, 0:1])
                s_qk = pb.act(kT_t[0:64, 512 * g:512 * (g + 1)], qkp[64:128, :], AF.Identity,
                              bias=qkvb_c[64:128, 0:1], inc=("SE", 1))
                pb.wait("tensor", "SE", s_qk)
                for kk in range(4):
                    s = 4 * g + kk
                    for j in range(NCH):
                        pb.mm(vp[:, 0:64], xnbf[:, S * j + 128 * s:S * j + 128 * (s + 1)],
                              qkvT_sb[:, 192 * j + 128:192 * (j + 1)], start=(j == 0), stop=False)
                    t_vv = pb.mm(vp[:, 0:64], ones_r[0:1, :], qkvb_vr[0:1, :], start=False, stop=True,
                                 inc=("TE", 1))
                    pb.wait("vector", "TE", t_vv)
                    v_vv = pb.copy(v1_sb[:, 64 * s:64 * (s + 1)], vp[:, 0:64], inc=("VE", 1))
                    pb.wait("tensor", "VE", v_vv)
        pb.barrier()
        if DEBUG_TAPS and sq == 0:
            pb.dma(dbg_q_d[:, :], qT_t[0:64, :], store=True)
            pb.barrier()
            pb.dma(dbg_k_d[:, :], kT_t[0:64, :], store=True)
            pb.barrier()
            pb.dma(dbg_v_d[:, :], v1_sb[:, :], store=True)
            pb.barrier()
            pb.dma(dbg_xn_d[:, :], xnbf[:, :], store=True)
            pb.barrier()
            pb.dma(dbg_mr_d[0:1, :], m1row[0:1, :], store=True)
            pb.barrier()
            pb.dma(dbg_mr_d[1:2, :], r1row[0:1, :], store=True)
            pb.barrier()
            pb.dma(dbg_bc_d[:, :], qkvb_c[:, 0:1], store=True)
            pb.barrier()
            pb.dma(dbg_P_d[:, :], PT_sb[0:64, :], store=True)
            pb.barrier()
        r1_es.__exit__(None, None, None)

        emit_ffn(1)
        pb.barrier()
        if DEBUG_TAPS and sq == 0:
            pb.dma(dbg5_d[:, :], x1acc[:, :], store=True)
            pb.barrier()

        with contextlib.ExitStack() as p7:
            sc_sb = p7.enter_context(sb("p7_sc" + tag, [128, 2 * 128], BF16))
            oh5 = p7.enter_context(sb("p7_oh" + tag, [128, 64], BF16))
            scp = p7.enter_context(nc.psum_tensor("p7_scp" + tag, [128, 2 * 128], F32))
            ohp = p7.enter_context(nc.psum_tensor("p7_ohp" + tag, [128, 64], F32))
            trp7 = p7.enter_context(nc.psum_tensor("p7_trp" + tag, [64, 128], BF16))
            ppp = p7.enter_context(nc.psum_tensor("p7_ppp" + tag, [128, 512], F32))

            for qs in range(NSUB):
                for kt in range(qs + 1):
                    pk = kt % 2
                    t_s = pb.mm(scp[:, 128 * pk:128 * (pk + 1)], kT_t[0:64, 128 * kt:128 * (kt + 1)],
                                qT_t[0:64, 128 * qs:128 * (qs + 1)], inc=("TE", 1))
                    if kt == qs:
                        pb.wait("vector", "TE", t_s)
                        v_s = pb.tt(sc_sb[:, 128 * pk:128 * (pk + 1)], scp[:, 128 * pk:128 * (pk + 1)],
                                    trimask[:, :], ALU.mult, inc=("VE", 1))
                        pb.wait("tensor", "VE", v_s)
                    else:
                        pb.wait("scalar", "TE", t_s)
                        s_s = pb.act(sc_sb[:, 128 * pk:128 * (pk + 1)], scp[:, 128 * pk:128 * (pk + 1)],
                                     AF.Identity, inc=("SE", 1))
                        pb.wait("tensor", "SE", s_s)
                    pb.mm(ohp[:, :], sc_sb[:, 128 * pk:128 * (pk + 1)], v1_sb[:, 64 * kt:64 * (kt + 1)],
                          start=(kt == 0), stop=(kt == qs))
                t_oh = pb.mark("tensor")
                pb.wait("vector", "TE", t_oh)
                v_oh = pb.copy(oh5[:, :], ohp[:, :], inc=("VE", 1))
                pb.wait("tensor", "VE", v_oh)
                t_ot = pb.tr(trp7[0:64, :], oh5[:, :], ident[:, :], inc=("TE", 1))
                pb.wait("vector", "TE", t_ot)
                v_ot = pb.copy(ohT[0:64, 128 * qs:128 * (qs + 1)], trp7[0:64, :], inc=("VE", 1))
                pb.wait("tensor", "VE", v_ot)
            if DEBUG_TAPS and sq == 0:
                pb.barrier()
                pb.dma(dbg_oh_d[:, :], ohT[0:64, :], store=True)
                pb.barrier()
            for g in range(4):
                for oc in range(NCH):
                    t_p = pb.mm(ppp[:, :], PT_sb[0:64, 128 * oc:128 * (oc + 1)],
                                ohT[0:64, 512 * g:512 * (g + 1)], inc=("TE", 1))
                    pb.wait("vector", "TE", t_p)
                    xsl = x1acc[:, S * oc + 512 * g:S * oc + 512 * (g + 1)]
                    v_p = pb.stt(xsl, ppp[:, :], projb_c[:, oc:oc + 1], xsl, ALU.add, ALU.add, inc=("VE", 1))
                    pb.wait("tensor", "VE", v_p)
        pb.barrier()
        attn1_es.__exit__(None, None, None)
        prow_es.__exit__(None, None, None)

        with contextlib.ExitStack() as p9:
            outt = p9.enter_context(sb("p9_out" + tag, [128, D], F32))
            pso = p9.enter_context(nc.psum_tensor("p9_ps" + tag, [128, D], F32))
            for s in range(NSUB):
                t_o = None
                for c in range(NCH):
                    t_o = pb.tr(pso[:, 128 * c:128 * (c + 1)], x1acc[:, S * c + 128 * s:S * c + 128 * (s + 1)],
                                identf[:, :], inc=("TE", 1) if c == NCH - 1 else None)
                pb.wait("vector", "TE", t_o)
                v_o = pb.copy(outt[:, :], pso[:, :], inc=("VE", 1))
                pb.wait("sync", "VE", v_o)
                st_o = pb.dma(ys_d[128 * s:128 * (s + 1), :], outt[:, :], store=True)
                pb.wait("vector", "DS", st_o)
                pb.wait("tensor", "DS", st_o)
        pb.barrier()

    pb.wait("sync", "DL", pb.cnt["DL"])
    pb.wait("sync", "DS", pb.cnt["DS"])
    pb.wait("sync", "GD", pb.cnt["GD"])

    with nc.Block() as block:
        @block.sync
        def _(e):
            for fn in pb.q["sync"]:
                fn(e)

        @block.tensor
        def _(e):
            for fn in pb.q["tensor"]:
                fn(e)

        @block.vector
        def _(e):
            for fn in pb.q["vector"]:
                fn(e)

        @block.scalar
        def _(e):
            for fn in pb.q["scalar"]:
                fn(e)

        @block.gpsimd
        def _(e):
            for fn in pb.q["gpsimd"]:
                fn(e)

    es.__exit__(None, None, None)
    return nc


_NC_CACHE = None


def kernel(**inputs):
    global _NC_CACHE
    if _NC_CACHE is None:
        _NC_CACHE = build_program()
    nc = _NC_CACHE
    x = np.ascontiguousarray(np.asarray(inputs["L0_attn_x"], dtype=np.float32))
    in_maps = []
    for core in range(8):
        m = {"L0_attn_x": x[2 * core:2 * core + 2]}
        for k, v in inputs.items():
            if k != "L0_attn_x":
                m[k] = np.ascontiguousarray(np.asarray(v, dtype=np.float32))
        in_maps.append(m)
    res = run_bass_kernel_spmd(nc, in_maps, core_ids=list(range(8)))
    out = np.concatenate([r["y"] for r in res.results], axis=0)
    return out.reshape(16, S, D)


# version 12
# speedup vs baseline: 1.0686x; 1.0686x over previous
import numpy as np
import ml_dtypes
import contextlib
import concourse.bass as bass
import concourse.mybir as mybir
from concourse.bass_utils import run_bass_kernel_spmd

F32 = mybir.dt.float32
BF16 = mybir.dt.bfloat16
FP8 = mybir.dt.float8e4
I32 = mybir.dt.int32
I16 = mybir.dt.int16
ALU = mybir.AluOpType
AF = mybir.ActivationFunctionType
AX = mybir.AxisListType.X

PI = float(np.float32(np.pi))
PIH = float(np.float32(np.pi / 2))
C2PI_HI = float(np.float32(2 * np.pi))
C2PI_LO = float(np.float64(2 * np.pi) - np.float64(np.float32(2 * np.pi)))
PIH64 = float(np.pi / 2)
PISAFE = float(np.nextafter(np.float32(np.pi), np.float32(0)))
ZCLIP = float(np.float64(1e-6) * 128)

D = 1024
NCH = 8
INNER = 4096
ICH = 32
S = 2048
NSUB = 16
H = 16
FD = 128
HD = 64
NSEQ = 2

ENGINES = ("sync", "tensor", "vector", "scalar", "gpsimd")
DEBUG_TAPS = False
DEBUG_TAP_GROUP = 0


def _iv(ap):
    t = ap.tensor
    off = int(ap.offset)
    span = 1
    for p in ap.ap:
        st, cnt = int(p[0]), int(p[1])
        span += (cnt - 1) * abs(st)
    return (t.name, off, off + span)


class PB:
    def __init__(self, nc, es):
        self.nc = nc
        self.q = {e: [] for e in ENGINES}
        self.sem = {}
        self.cnt = {}
        for name in ("DL", "DS", "GD", "TE", "VE", "SE", "GE"):
            self.sem[name] = es.enter_context(nc.semaphore(name))
            self.cnt[name] = 0
        self.esem = {"tensor": "TE", "vector": "VE", "scalar": "SE", "gpsimd": "GE"}
        self.dirty = {e: ([], []) for e in ENGINES}
        self.last = {}

    def _emit(self, eng, fn):
        self.q[eng].append(fn)

    def _track(self, eng, reads, writes):
        if eng in ("sync", "tensor"):
            return
        R, W = self.dirty[eng]
        rv = [_iv(a) for a in reads]
        wv = [_iv(a) for a in writes]

        def ovl(a, b):
            return a[0] == b[0] and a[1] < b[2] and b[1] < a[2]
        if any(ovl(r, w) for r in rv for w in W) or any(ovl(w, x) for w in wv for x in W + R):
            self._emit(eng, lambda e: e.drain())
            self.dirty[eng] = ([], [])
            R, W = self.dirty[eng]
        R.extend(rv)
        W.extend(wv)

    def op(self, eng, fn, reads=(), writes=(), inc=None, skip_track=False):
        if not skip_track:
            self._track(eng, list(reads), list(writes))
        if inc is None:
            idx = len(self.q[eng])
            self._emit(eng, fn)
            self.last[eng] = (idx, fn, None, None)
            return None
        nm, n = inc
        sem = self.sem[nm]
        self.cnt[nm] += n
        val = self.cnt[nm]
        idx = len(self.q[eng])
        self._emit(eng, lambda e: fn(e).then_inc(sem, n))
        self.last[eng] = (idx, fn, nm, val)
        return val

    def wait(self, eng, nm, val=None):
        if isinstance(val, tuple):
            nm, val = val
        elif val is None:
            if nm is None or not isinstance(nm, tuple):
                return
            nm, val = nm
        if val is None or val <= 0:
            return
        sem = self.sem[nm]
        self._emit(eng, lambda e: e.wait_ge(sem, val))

    def mark(self, eng):
        nm = self.esem[eng]
        idx, fn, lsem, lval = self.last.get(eng, (None, None, None, None))
        if idx is None:
            return (nm, self.cnt[nm])
        if lsem is not None:
            return (lsem, lval)
        sem = self.sem[nm]
        self.cnt[nm] += 1
        val = self.cnt[nm]
        self.q[eng][idx] = (lambda e, f=fn, s=sem: f(e).then_inc(s, 1))
        self.last[eng] = (idx, fn, nm, val)
        return (nm, val)

    def barrier(self):
        toks = {e: self.mark(e) for e in ("tensor", "vector", "scalar", "gpsimd")}
        dl, ds, gd = self.cnt["DL"], self.cnt["DS"], self.cnt["GD"]
        for e in ENGINES:
            for e2 in ("tensor", "vector", "scalar", "gpsimd"):
                if e2 != e:
                    self.wait(e, toks[e2])
            self.wait(e, "DL", dl)
            self.wait(e, "DS", ds)
            self.wait(e, "GD", gd)
            self.dirty[e] = ([], [])

    def dma(self, out, in_, eng="sync", store=False):
        nm = "GD" if eng == "gpsimd" else ("DS" if store else "DL")
        self.wait(eng, nm, self.cnt[nm])
        val = self.op(eng, lambda e: e.dma_start(out=out, in_=in_), inc=(nm, 16), skip_track=True)
        return val

    def ts(self, out, in0, s1, s2=None, op0=ALU.mult, op1=ALU.bypass, inc=None):
        reads = [in0] + [a for a in (s1, s2) if hasattr(a, "tensor")]
        if s2 is None:
            return self.op("vector", lambda e: e.tensor_scalar(out=out, in0=in0, scalar1=s1, scalar2=None, op0=op0),
                           reads=reads, writes=[out], inc=inc)
        return self.op("vector", lambda e: e.tensor_scalar(out=out, in0=in0, scalar1=s1, scalar2=s2, op0=op0, op1=op1),
                       reads=reads, writes=[out], inc=inc)

    def tt(self, out, in0, in1, alu, inc=None):
        return self.op("vector", lambda e: e.tensor_tensor(out=out, in0=in0, in1=in1, op=alu),
                       reads=[in0, in1], writes=[out], inc=inc)

    def stt(self, out, in0, s, in1, op0, op1, accum=None, inc=None):
        reads = [in0, in1] + ([s] if hasattr(s, "tensor") else [])
        writes = [out] + ([accum] if accum is not None else [])
        return self.op("vector", lambda e: e.scalar_tensor_tensor(out=out, in0=in0, scalar=s, in1=in1,
                                                                  op0=op0, op1=op1, accum_out=accum),
                       reads=reads, writes=writes, inc=inc)

    def copy(self, out, in_, inc=None):
        return self.op("vector", lambda e: e.tensor_copy(out=out, in_=in_), reads=[in_], writes=[out], inc=inc)

    def red(self, out, in_, alu, absval=False, inc=None):
        return self.op("vector", lambda e: e.tensor_reduce(out=out, in_=in_, axis=AX, op=alu,
                                                           apply_absolute_value=absval),
                       reads=[in_], writes=[out], inc=inc)

    def recip(self, out, in_, inc=None):
        return self.op("vector", lambda e: e.reciprocal(out=out, in_=in_), reads=[in_], writes=[out], inc=inc)

    def act(self, out, in_, func, bias=0.0, scale=1.0, accum=None, inc=None):
        reads = [in_] + [a for a in (bias, scale) if hasattr(a, "tensor")]
        writes = [out] + ([accum] if accum is not None else [])
        return self.op("scalar", lambda e: e.activation(out=out, in_=in_, func=func, bias=bias, scale=scale,
                                                        accum_out=accum),
                       reads=reads, writes=writes, inc=inc)

    def mm(self, out, lhsT, rhs, start=True, stop=True, inc=None):
        return self.op("tensor", lambda e: e.matmul(out, lhsT, rhs, start=start, stop=stop),
                       inc=inc, skip_track=True)

    def tr(self, out, in_, ident, inc=None):
        return self.op("tensor", lambda e: e.transpose(out, in_, ident), inc=inc, skip_track=True)

    def memset(self, ap, val, inc=None):
        return self.op("gpsimd", lambda e: e.memset(ap, val), reads=[], writes=[ap], inc=inc)

    def affsel(self, ap, cmp, fill, pattern, chmul, inc=None):
        return self.op("gpsimd", lambda e: e.affine_select(out=ap, in_=ap, compare_op=cmp, fill=fill,
                                                           base=0, pattern=pattern, channel_multiplier=chmul),
                       reads=[ap], writes=[ap], inc=inc)


def build_program():
    nc = bass.Bass("TRN2", target_bir_lowering=False)
    es = contextlib.ExitStack()
    es.__enter__()

    dt_in = {}
    x0_d = nc.dram_tensor("L0_attn_x", [NSEQ, S, D], F32, kind="ExternalInput").ap()
    for name, shape in [
        ("L0_attn_Wq", [D, D]), ("L0_attn_Wk", [D, D]), ("L0_attn_Wv", [D, D]),
        ("L0_attn_Wo", [D, D]), ("L0_attn_rf", [HD, FD]),
        ("L0_ffn_w_gate", [INNER, D]), ("L0_ffn_w_up", [INNER, D]), ("L0_ffn_w_down", [D, INNER]),
        ("L1_attn_qkv_w", [192, D]), ("L1_attn_qkv_b", [192]),
        ("L1_attn_proj_w", [D, D]), ("L1_attn_proj_b", [D]),
        ("L1_ffn_w1", [INNER, D]), ("L1_ffn_b1", [INNER]),
        ("L1_ffn_w2", [D, INNER]), ("L1_ffn_b2", [D]),
    ]:
        dt_in[name] = nc.dram_tensor(name, shape, F32, kind="ExternalInput").ap()
    y_d = nc.dram_tensor("y", [NSEQ, S, D], F32, kind="ExternalOutput").ap()

    wgT_d = nc.dram_tensor("wgT_s", [NCH, 128, NCH, 4, 128], FP8).ap()
    wuT_d = nc.dram_tensor("wuT_s", [NCH, 128, NCH, 4, 128], FP8).ap()
    w1T_d = nc.dram_tensor("w1T_s", [NCH, 128, NCH, 4, 128], FP8).ap()
    wdT_d = nc.dram_tensor("wdT_s", [ICH, 128, NCH, 128], FP8).ap()
    w2T_d = nc.dram_tensor("w2T_s", [ICH, 128, NCH, 128], FP8).ap()
    wqrf_d = nc.dram_tensor("wqrf_s", [NCH, 128, H, 128], BF16).ap()
    wkrf_d = nc.dram_tensor("wkrf_s", [NCH, 128, H, 128], BF16).ap()
    if DEBUG_TAPS:
        dbg1_d = nc.dram_tensor("dbg_ffn0", [128, NCH * S], F32).ap()
        dbg2_d = nc.dram_tensor("dbg_x1", [128, NCH * S], F32).ap()
        dbg3_d = nc.dram_tensor("dbg_kvz", [128, H * 65], BF16).ap()
        dbg_xq_d = nc.dram_tensor("dbg_xq", [128, NCH * S], BF16).ap()
        dbg_h_d = nc.dram_tensor("dbg_h", [128, 2 * INNER], BF16).ap()
        dbg_tcf_d = nc.dram_tensor("dbg_tcf", [128, 16], F32).ap()
        dbg_hqT_d = nc.dram_tensor("dbg_hqT", [128, ICH * 256], BF16).ap()
        dbg5_d = nc.dram_tensor("dbg_xf1", [128, NCH * S], F32).ap()
        dbg_q_d = nc.dram_tensor("dbg_q", [64, S], BF16).ap()
        dbg_k_d = nc.dram_tensor("dbg_k", [64, S], BF16).ap()
        dbg_v_d = nc.dram_tensor("dbg_v", [128, NSUB * 64], BF16).ap()
        dbg_oh_d = nc.dram_tensor("dbg_oh", [64, S], BF16).ap()
        dbg_xn_d = nc.dram_tensor("dbg_xn", [128, NCH * S], BF16).ap()
        dbg_mr_d = nc.dram_tensor("dbg_mr", [2, S], F32).ap()
        dbg_bc_d = nc.dram_tensor("dbg_bc", [128, 1], F32).ap()
        dbg_P_d = nc.dram_tensor("dbg_P", [64, 1024], BF16).ap()

    x1acc = es.enter_context(nc.sbuf_tensor("x1acc", [128, NCH * S], F32))
    xnbf = es.enter_context(nc.sbuf_tensor("xnbf", [128, NCH * S], BF16))
    ident = es.enter_context(nc.sbuf_tensor("ident", [128, 128], BF16))
    identf = es.enter_context(nc.sbuf_tensor("identf", [128, 128], F32))
    trimask = es.enter_context(nc.sbuf_tensor("trimask", [128, 128], BF16))
    ones_r = es.enter_context(nc.sbuf_tensor("ones_r", [1, 128], BF16))
    ones_rf = es.enter_context(nc.sbuf_tensor("ones_rf", [1, 128], F32))
    ones_c = es.enter_context(nc.sbuf_tensor("ones_c", [128, 1], BF16))
    one11 = es.enter_context(nc.sbuf_tensor("one11", [1, 1], F32))
    cc = es.enter_context(nc.sbuf_tensor("cc", [128, 12], F32))
    CHALF, CM1, CPIH2, CPIHALL, CP1, CM2, CP2, CM3, CP3 = 0, 1, 2, 3, 4, 5, 6, 7, 8
    qkvT_sb = es.enter_context(nc.sbuf_tensor("qkvT_sb", [128, NCH * 192], BF16))
    PT_sb = es.enter_context(nc.sbuf_tensor("PT_sb", [64, D], BF16))
    qkvb_c = es.enter_context(nc.sbuf_tensor("qkvb_c", [128, 1], F32))
    qkvb_vr = es.enter_context(nc.sbuf_tensor("qkvb_vr", [1, 64], BF16))
    projb_c = es.enter_context(nc.sbuf_tensor("projb_c", [128, NCH], F32))
    b1row = es.enter_context(nc.sbuf_tensor("b1row", [1, INNER], BF16))
    b2row = es.enter_context(nc.sbuf_tensor("b2row", [1, D], BF16))
    wmean2 = es.enter_context(nc.sbuf_tensor("wmean2", [128, 5], F32))
    MG, MU, MD, M1, M2 = range(5)
    invs_l0 = es.enter_context(nc.sbuf_tensor("invs_l0", [128, NSUB], F32))

    pb = PB(nc, es)

    def round_chain(w_ap, Ri_ap, d_ap, s1_ap, s2_ap, t12_ap, out_ap, wide=False):
        pb.copy(Ri_ap, w_ap)
        pb.tt(d_ap, w_ap, Ri_ap, ALU.subtract)
        js = {False: [-2, -1, 0, 1], "mid": [-3, -2, -1, 0, 1, 2, 3],
              True: [-3, -2, -1, 0, 1, 2, 3]}[wide]
        pb.ts(s1_ap, d_ap, float(js[0]), float(js[0] - 1), op0=ALU.is_gt, op1=ALU.add)
        for j in js[1:]:
            pb.stt(s1_ap, d_ap, float(j), s1_ap, ALU.is_gt, ALU.add)
        pb.tt(out_ap, s1_ap, Ri_ap, ALU.add)

    def wrap_sin(y_ap, n1_ap, out_ap):
        pb.ts(n1_ap, y_ap, PI, None, op0=ALU.is_gt)
        pb.stt(n1_ap, y_ap, -PI, n1_ap, ALU.is_gt, ALU.add)
        pb.ts(n1_ap, n1_ap, -1.0, None, op0=ALU.add)
        pb.stt(y_ap, n1_ap, -C2PI_HI, y_ap, ALU.mult, ALU.add)
        pb.stt(y_ap, n1_ap, -C2PI_LO, y_ap, ALU.mult, ALU.add)
        pb.ts(y_ap, y_ap, -PISAFE, PISAFE, op0=ALU.max, op1=ALU.min)
        v_w = pb.mark("vector")
        pb.wait("scalar", "VE", v_w)
        s_k = pb.act(out_ap, y_ap, AF.Sin, inc=("SE", 1))
        return s_k

    pb.memset(ident[:, :], 0.0)
    pb.affsel(ident[:, :], ALU.not_equal, 1.0, [[-1, 128]], 1)
    pb.memset(identf[:, :], 0.0)
    pb.affsel(identf[:, :], ALU.not_equal, 1.0, [[-1, 128]], 1)
    pb.memset(trimask[:, :], 0.0)
    pb.affsel(trimask[:, :], ALU.is_gt, 1.0, [[-1, 128]], 1)
    pb.memset(ones_r[:, :], 1.0)
    pb.memset(ones_rf[:, :], 1.0)
    pb.memset(ones_c[:, :], 1.0)
    pb.memset(one11[:, :], 1.0)
    pb.memset(cc[:, CHALF:CHALF + 1], 0.5)
    pb.memset(cc[:, CM1:CM1 + 1], -1.0)
    pb.memset(cc[0:64, CPIH2:CPIH2 + 1], 0.0)
    pb.memset(cc[64:128, CPIH2:CPIH2 + 1], 0.0)
    pb.memset(cc[:, CPIHALL:CPIHALL + 1], 0.0)
    pb.memset(cc[:, CP1:CP1 + 1], 1.0)
    pb.memset(cc[:, CM2:CM2 + 1], -2.0)
    pb.memset(cc[:, CP2:CP2 + 1], 2.0)
    pb.memset(cc[:, CM3:CM3 + 1], -3.0)
    pb.memset(cc[:, CP3:CP3 + 1], 3.0)
    pb.barrier()
    pb.ts(cc[0:64, CPIH2:CPIH2 + 1], cc[0:64, CPIH2:CPIH2 + 1], PIH64, None, op0=ALU.add)
    pb.ts(cc[:, CPIHALL:CPIHALL + 1], cc[:, CPIHALL:CPIHALL + 1], PIH64, None, op0=ALU.add)
    pb.barrier()

    with contextlib.ExitStack() as st_es:
        sb = nc.sbuf_tensor
        rf_sb = st_es.enter_context(sb("rf_sb", [64, FD], BF16))
        wt = st_es.enter_context(sb("st_wt", [128, INNER], F32))
        wt2 = st_es.enter_context(sb("st_wt2", [128, INNER], F32))
        s1t = st_es.enter_context(sb("st_s1", [128, INNER], BF16))
        s2t = st_es.enter_context(sb("st_s2", [128, INNER], BF16))
        ttile = st_es.enter_context(sb("st_ttile", [128, INNER], BF16))
        acc_c = st_es.enter_context(sb("st_acc", [128, ICH], F32))
        tiny = st_es.enter_context(sb("st_tiny", [128, 4], F32))
        rowt = st_es.enter_context(sb("st_rowt", [1, 128], F32))
        sc11 = st_es.enter_context(sb("st_sc11", [1, 4], F32))
        qkw = st_es.enter_context(sb("st_qkw", [128, D], BF16))
        qkw2 = st_es.enter_context(sb("st_qkw2", [64, D], BF16))
        pc64 = st_es.enter_context(sb("st_pc64", [128, 64], BF16))
        wkT = st_es.enter_context(sb("st_wkT", [64, 128], BF16))
        wrow = st_es.enter_context(sb("st_wrow", [128, FD], BF16))
        ps_b = st_es.enter_context(nc.psum_tensor("st_psb", [128, INNER], BF16))
        ps_f = st_es.enter_context(nc.psum_tensor("st_psf", [128, 256], F32))
        ps_g = st_es.enter_context(nc.psum_tensor("st_psg", [128, 128], BF16))
        ps_h = st_es.enter_context(nc.psum_tensor("st_psh", [128, 128], F32))
        ps_g2 = st_es.enter_context(nc.psum_tensor("st_psg2", [128, 128], F32))

        prow = wt[0:1, :]
        pwt = wt[:, 0:D]
        ld_rf = pb.dma(rf_sb[:, :], dt_in["L0_attn_rf"][:, :], eng="gpsimd")
        pb.dma(qkw[:, :], dt_in["L1_attn_qkv_w"][0:128, :], eng="gpsimd")
        ld_q2 = pb.dma(qkw2[:, :], dt_in["L1_attn_qkv_w"][128:192, :], eng="gpsimd")
        ldb = pb.dma(prow[0:1, 0:192], dt_in["L1_attn_qkv_b"][None, :])
        pb.wait("vector", "DL", ldb)
        pb.copy(qkvb_vr[0:1, :], prow[0:1, 128:192])
        pb.wait("tensor", "DL", ldb)
        t0 = pb.mm(ps_h[:, 0:1], prow[0:1, 0:128], one11[:, :], inc=("TE", 1))
        pb.wait("vector", "TE", t0)
        v0 = pb.copy(qkvb_c[:, 0:1], ps_h[:, 0:1], inc=("VE", 1))
        pb.wait("sync", "TE", pb.cnt["TE"])
        pb.wait("sync", pb.mark("vector"))
        ldb2 = pb.dma(prow[0:1, 0:D], dt_in["L1_attn_proj_b"][None, :])
        pb.wait("tensor", "DL", ldb2)
        pb.wait("tensor", "VE", v0)
        t1 = None
        for c in range(NCH):
            t1 = pb.mm(ps_f[:, c:c + 1], prow[0:1, 128 * c:128 * (c + 1)], one11[:, :], inc=("TE", 1))
        pb.wait("vector", "TE", t1)
        pb.copy(projb_c[:, :], ps_f[:, 0:NCH])
        pb.wait("sync", "TE", pb.cnt["TE"])
        pb.wait("sync", pb.mark("vector"))
        ldb3 = pb.dma(prow[0:1, 0:INNER], dt_in["L1_ffn_b1"][None, :])
        pb.wait("vector", "DL", ldb3)
        pb.copy(b1row[0:1, :], prow[0:1, :])
        pb.wait("sync", pb.mark("vector"))
        ldb4 = pb.dma(prow[0:1, 0:D], dt_in["L1_ffn_b2"][None, :])
        pb.wait("vector", "DL", ldb4)
        pb.copy(b2row[0:1, :], prow[0:1, 0:D])

        pb.wait("tensor", "GD", ld_q2)
        for i in range(NCH):
            pb.tr(ps_g[:, :], qkw[:, 128 * i:128 * (i + 1)], ident[:, :])
            t2 = pb.tr(ps_b[:, 0:64], qkw2[0:64, 128 * i:128 * (i + 1)], ident[0:64, 0:64], inc=("TE", 1))
            pb.wait("vector", "TE", t2)
            pb.copy(qkvT_sb[:, 192 * i:192 * i + 128], ps_g[:, :])
            v1 = pb.copy(qkvT_sb[:, 192 * i + 128:192 * (i + 1)], ps_b[:, 0:64], inc=("VE", 1))
            pb.wait("tensor", "VE", v1)

        pb.wait("sync", pb.mark("vector"))
        pb.wait("sync", "TE", pb.cnt["TE"])
        for c in range(NCH):
            ldp = pb.dma(pwt[:, :], dt_in["L1_attn_proj_w"][128 * c:128 * (c + 1), :])
            pb.wait("vector", "DL", ldp)
            pv = pwt[:, :].rearrange("p (a b) -> p b a", a=H)
            def _pred(e, pv=pv, pc=pc64):
                with nc.allow_low_precision(reason="P col-block sum written bf16 (weight operand)"):
                    return e.tensor_reduce(out=pc[:, :], in_=pv, axis=AX, op=ALU.add,
                                           apply_absolute_value=False)
            v2 = pb.op("vector", _pred, reads=[pv], writes=[pc64[:, :]], inc=("VE", 1))
            pb.wait("tensor", "VE", v2)
            t3 = pb.tr(ps_b[0:64, 0:128], pc64[:, :], ident[:, :], inc=("TE", 1))
            pb.wait("vector", "TE", t3)
            v3 = pb.copy(PT_sb[0:64, 128 * c:128 * (c + 1)], ps_b[0:64, 0:128], inc=("VE", 1))
            pb.wait("sync", "VE", v3)

        pb.wait("tensor", "GD", ld_rf)
        pb.wait("tensor", pb.mark("vector"))
        for (wname, wdst) in (("L0_attn_Wq", wqrf_d), ("L0_attn_Wk", wkrf_d)):
            for i in range(NCH):
                pb.wait("gpsimd", "GD", pb.cnt["GD"])
                pb.wait("gpsimd", "TE", pb.cnt["TE"])
                ldw = pb.dma(qkw[:, :], dt_in[wname][128 * i:128 * (i + 1), :], eng="gpsimd")
                pb.wait("tensor", "GD", ldw)
                for h in range(H):
                    t4 = pb.tr(ps_b[0:64, 0:128], qkw[:, 64 * h:64 * (h + 1)], ident[:, :], inc=("TE", 1))
                    pb.wait("vector", "TE", t4)
                    v4 = pb.copy(wkT[0:64, :], ps_b[0:64, 0:128], inc=("VE", 1))
                    pb.wait("tensor", "VE", v4)
                    t5 = pb.mm(ps_g2[:, :], wkT[0:64, :], rf_sb[0:64, :], inc=("TE", 1))
                    pb.wait("vector", "TE", t5)
                    v5 = pb.copy(wrow[:, :], ps_g2[:, :], inc=("VE", 1))
                    pb.wait("sync", "VE", v5)
                    st_w = pb.dma(wdst[i, :, h, :], wrow[:, :], store=True)
                    pb.wait("vector", "DS", st_w)
                pb.wait("gpsimd", "TE", pb.cnt["TE"])
        pb.barrier()

        ternspecs = [
            ("L0_ffn_w_gate", wgT_d, INNER, D, MG),
            ("L0_ffn_w_up", wuT_d, INNER, D, MU),
            ("L1_ffn_w1", w1T_d, INNER, D, M1),
            ("L0_ffn_w_down", wdT_d, D, INNER, MD),
            ("L1_ffn_w2", w2T_d, D, INNER, M2),
        ]
        for wname, wdst, NOUT, NIN, mcol in ternspecs:
            n_ot = NOUT // 128
            n_j = NIN // 128
            numel_inv = float(1.0 / (NOUT * NIN))
            pb.wait("scalar", "DS", pb.cnt["DS"])
            pb.wait("scalar", "VE", pb.cnt["VE"])
            pb.wait("scalar", "GD", pb.cnt["GD"])
            wts = (wt, wt2)
            lda = {}

            def ld_blk(i):
                lda[i] = pb.dma(wts[i % 2][:, 0:NIN], dt_in[wname][128 * i:128 * (i + 1), :])
            pb.wait("sync", "SE", pb.cnt["SE"])
            pb.wait("sync", "VE", pb.cnt["VE"])
            ld_blk(0)
            s_a = None
            for i in range(n_ot):
                pb.wait("scalar", "DL", lda[i])
                if i + 1 < n_ot:
                    if s_a is not None:
                        pb.wait("sync", "SE", s_a)
                    ld_blk(i + 1)
                s_a = pb.act(ttile[:, 0:NIN], wts[i % 2][:, 0:NIN], AF.Abs, accum=acc_c[:, i:i + 1],
                             inc=("SE", 1))
            pb.wait("vector", "SE", pb.cnt["SE"])
            pb.red(tiny[:, 2:3], acc_c[:, 0:n_ot], ALU.add)
            v6 = pb.mark("vector")
            pb.wait("tensor", "VE", v6)
            t6 = pb.tr(ps_h[0:1, 0:128], tiny[:, 2:3], identf[:, :], inc=("TE", 1))
            pb.wait("vector", "TE", t6)
            pb.copy(rowt[0:1, 0:128], ps_h[0:1, 0:128])
            pb.red(sc11[0:1, 0:1], rowt[0:1, 0:128], ALU.add)
            pb.ts(sc11[0:1, 1:2], sc11[0:1, 0:1], numel_inv, 1e-5, ALU.mult, ALU.max)
            pb.ts(sc11[0:1, 2:3], sc11[0:1, 1:2], 0.5, None, ALU.mult)
            v7 = pb.mark("vector")
            pb.wait("tensor", "VE", v7)
            t7 = pb.mm(ps_h[:, 0:1], ones_rf[0:1, :], sc11[0:1, 2:3], inc=("TE", 1))
            pb.wait("vector", "TE", t7)
            pb.copy(tiny[:, 0:1], ps_h[:, 0:1])
            pb.ts(tiny[:, 1:2], tiny[:, 0:1], -1.0, None, op0=ALU.mult)
            v8 = pb.copy(wmean2[:, mcol:mcol + 1], ps_h[:, 0:1], inc=("VE", 1))
            pb.wait("scalar", "VE", v8)
            pb.wait("tensor", "VE", v8)
            pb.wait("sync", "SE", pb.cnt["SE"])
            ld_blk(0)
            s_b = None
            t8s = {}
            sts = {}
            for i in range(n_ot):
                wti = wts[i % 2]
                q = (i % 4) * D if NIN == D else 0
                pb.wait("scalar", "DL", lda[i])
                if i + 1 < n_ot:
                    if s_b is not None:
                        pb.wait("sync", "SE", s_b)
                    ld_blk(i + 1)
                if NIN == D and i >= 4:
                    pb.wait("scalar", "TE", t8s[i - 4])
                elif NIN != D and i >= 1:
                    pb.wait("scalar", "TE", t8s[i - 1])
                pb.act(s1t[:, q:q + NIN], wti[:, 0:NIN], AF.Sign, bias=tiny[:, 1:2])
                s_b = pb.act(s2t[:, q:q + NIN], wti[:, 0:NIN], AF.Sign, bias=tiny[:, 0:1], inc=("SE", 1))
                pb.wait("vector", "SE", s_b)
                v9 = pb.tt(s1t[:, q:q + NIN], s1t[:, q:q + NIN], s2t[:, q:q + NIN], ALU.add, inc=("VE", 1))
                pb.wait("tensor", "VE", v9)
                t8 = None
                for j in range(n_j):
                    t8 = pb.tr(ps_b[:, q + 128 * j:q + 128 * (j + 1)],
                               s1t[:, q + 128 * j:q + 128 * (j + 1)], ident[:, :],
                               inc=("TE", 1) if j == n_j - 1 else None)
                t8s[i] = t8
                pb.wait("vector", "TE", t8)
                if NIN == D:
                    if i >= 4:
                        pb.wait("vector", "GD", sts[i - 4])
                    v10 = pb.copy(ttile[:, q:q + NIN], ps_b[:, q:q + NIN], inc=("VE", 1))
                    pb.wait("gpsimd", "VE", v10)
                    sts[i] = pb.dma(wdst[i // 4, :, :, i % 4, :],
                                    ttile[:, q:q + NIN].rearrange("p (j q2) -> p j q2", j=n_j),
                                    eng="gpsimd", store=True)
                else:
                    v10 = pb.copy(ttile[:, 0:NIN], ps_b[:, 0:NIN], inc=("VE", 1))
                    pb.wait("gpsimd", "VE", v10)
                    st_t = pb.dma(wdst[0:n_j, :, i, :].rearrange("j p q -> p j q"),
                                  ttile[:, 0:NIN].rearrange("p (j q) -> p j q", j=n_j),
                                  eng="gpsimd", store=True)
                    pb.wait("vector", "GD", st_t)
        pb.barrier()

    for sq in range(NSEQ):
        xs_d = x0_d[sq]
        ys_d = y_d[sq]
        tag = "_s%d" % sq
        sb = nc.sbuf_tensor

        xq_es = contextlib.ExitStack()
        xq_es.__enter__()
        xq1T = xq_es.enter_context(sb("xq1T" + tag, [128, NCH * S], BF16))
        with contextlib.ExitStack() as p0:
            x0t = p0.enter_context(sb("e_x0" + tag, [128, D], F32))
            junk = p0.enter_context(sb("e_junk" + tag, [128, D], BF16))
            tc = p0.enter_context(sb("e_tiny" + tag, [128, 12], F32))
            wtile = p0.enter_context(sb("e_w" + tag, [128, D], F32))
            Ri = p0.enter_context(sb("e_Ri" + tag, [128, D], I32))
            dtile = p0.enter_context(sb("e_d" + tag, [128, D], BF16))
            s1q = p0.enter_context(sb("e_s1" + tag, [128, D], BF16))
            s2q = p0.enter_context(sb("e_s2" + tag, [128, D], BF16))
            t12 = p0.enter_context(sb("e_t12" + tag, [128, D], BF16))
            xn_t = p0.enter_context(sb("e_xn" + tag, [128, D], BF16))
            xq_t = p0.enter_context(sb("e_xq" + tag, [128, D], BF16))
            psA = p0.enter_context(nc.psum_tensor("e_psA" + tag, [128, D], F32))
            psB = p0.enter_context(nc.psum_tensor("e_psB" + tag, [128, D], BF16))
            psC = p0.enter_context(nc.psum_tensor("e_psC" + tag, [128, D], BF16))

            for s in range(NSUB):
                ld = pb.dma(x0t[:, :], xs_d[128 * s:128 * (s + 1), :])
                pb.wait("vector", "DL", ld)
                pb.wait("scalar", "DL", ld)
                pb.red(tc[:, 9:10], x0t[:, :], ALU.add)
                s_sq = pb.act(junk[:, :], x0t[:, :], AF.Square, accum=tc[:, 10:11], inc=("SE", 1))
                pb.red(tc[:, 11:12], x0t[:, :], ALU.max)
                pb.red(tc[:, 6:7], x0t[:, :], ALU.min)
                pb.wait("vector", "SE", s_sq)
                pb.ts(tc[:, 0:1], tc[:, 9:10], 1.0 / 1024, None)
                pb.tt(tc[:, 2:3], tc[:, 0:1], tc[:, 0:1], ALU.mult)
                pb.stt(tc[:, 3:4], tc[:, 10:11], 1.0 / 1024, tc[:, 2:3], ALU.mult, ALU.subtract)
                pb.ts(tc[:, 3:4], tc[:, 3:4], 1e-5, None, op0=ALU.add)
                v_ve = pb.mark("vector")
                pb.wait("scalar", "VE", v_ve)
                pb.act(tc[:, 4:5], tc[:, 3:4], AF.Ln)
                s_r = pb.act(tc[:, 1:2], tc[:, 4:5], AF.Exp, scale=-0.5, inc=("SE", 1))
                pb.wait("vector", "SE", s_r)
                pb.tt(tc[:, 9:10], tc[:, 11:12], tc[:, 0:1], ALU.subtract)
                pb.tt(tc[:, 10:11], tc[:, 0:1], tc[:, 6:7], ALU.subtract)
                pb.tt(tc[:, 9:10], tc[:, 9:10], tc[:, 10:11], ALU.max)
                pb.tt(tc[:, 9:10], tc[:, 9:10], tc[:, 1:2], ALU.mult)
                pb.ts(tc[:, 9:10], tc[:, 9:10], 1e-5, None, op0=ALU.max)
                pb.recip(tc[:, 10:11], tc[:, 9:10])
                pb.ts(tc[:, 7:8], tc[:, 10:11], 127.0, None)
                pb.recip(tc[:, 8:9], tc[:, 7:8])
                pb.copy(invs_l0[:, s:s + 1], tc[:, 8:9])
                pb.tt(tc[:, 4:5], tc[:, 1:2], tc[:, 7:8], ALU.mult)
                pb.stt(tc[:, 5:6], tc[:, 0:1], -1.0, tc[:, 4:5], ALU.mult, ALU.mult)
                pb.ts(tc[:, 5:6], tc[:, 5:6], 0.5, None, op0=ALU.add)
                pb.stt(tc[:, 6:7], tc[:, 0:1], -1.0, tc[:, 1:2], ALU.mult, ALU.mult)
                v_st = pb.mark("vector")
                pb.wait("scalar", "VE", v_st)
                pb.act(xn_t[:, :], x0t[:, :], AF.Identity, scale=tc[:, 1:2], bias=tc[:, 6:7])
                s_w = pb.act(wtile[:, :], x0t[:, :], AF.Identity, scale=tc[:, 4:5], bias=tc[:, 5:6], inc=("SE", 1))
                pb.wait("vector", "SE", s_w)
                round_chain(wtile[:, :], Ri[:, :], dtile[:, :], s1q[:, :], s2q[:, :], t12[:, :], xq_t[:, :])
                v_xq = pb.mark("vector")
                pb.wait("tensor", "VE", v_xq)
                pb.wait("tensor", "SE", s_w)
                for c in range(NCH):
                    pb.tr(psA[:, 128 * c:128 * (c + 1)], x0t[:, 128 * c:128 * (c + 1)], identf[:, :])
                for c in range(NCH):
                    pb.tr(psB[:, 128 * c:128 * (c + 1)], xn_t[:, 128 * c:128 * (c + 1)], ident[:, :])
                t_tr = None
                for c in range(NCH):
                    t_tr = pb.tr(psC[:, 128 * c:128 * (c + 1)], xq_t[:, 128 * c:128 * (c + 1)], ident[:, :],
                                 inc=("TE", 1) if c == NCH - 1 else None)
                pb.wait("vector", "TE", t_tr)
                xa = x1acc[:, :].rearrange("p (c t) -> p c t", c=NCH)[:, :, 128 * s:128 * (s + 1)]
                pb.copy(xa, psA[:, :].rearrange("p (c t) -> p c t", c=NCH))
                xb_ = xnbf[:, :].rearrange("p (c t) -> p c t", c=NCH)[:, :, 128 * s:128 * (s + 1)]
                pb.copy(xb_, psB[:, :].rearrange("p (c t) -> p c t", c=NCH))
                xq1 = xq1T[:, :].rearrange("p (c t) -> p c t", c=NCH)[:, :, 128 * s:128 * (s + 1)]
                v_ev = pb.copy(xq1, psC[:, :].rearrange("p (c t) -> p c t", c=NCH), inc=("VE", 1))
                pb.wait("sync", "VE", v_ev)
                pb.wait("sync", "SE", pb.cnt["SE"])
                pb.wait("sync", "TE", t_tr)
                pb.wait("tensor", "VE", v_ev)
        pb.barrier()

        def emit_ffn(layer):
            ltag = tag + "_f%d" % layer
            with contextlib.ExitStack() as pf:
                wbufA = pf.enter_context(sb("f_wA" + ltag, [128, 2 * 4096], BF16))
                wbufD = wbufA
                h_sb = pf.enter_context(sb("f_h" + ltag, [128, 2 * INNER], BF16))
                hqT = pf.enter_context(sb("f_hqT" + ltag, [128, ICH * 256], BF16))
                wq5 = pf.enter_context(sb("f_w5" + ltag, [128, 1024], BF16))
                Ri5 = pf.enter_context(sb("f_Ri5" + ltag, [128, 1024], I16))
                d5 = wq5
                s15 = pf.enter_context(sb("f_s15" + ltag, [128, 1024], BF16))
                s25 = wq5
                dsc = pf.enter_context(sb("f_dsc" + ltag, [128, 512], F32))
                sgt = dsc
                t125 = d5
                rq5 = s15
                junk5 = dsc
                tcf = pf.enter_context(sb("f_tc" + ltag, [128, 16], F32))
                hsq = pf.enter_context(sb("f_hsq" + ltag, [128, NCH], F32))
                rowb = pf.enter_context(sb("f_rowb" + ltag, [1, 256], BF16))
                xqL = pf.enter_context(sb("f_xqL" + ltag, [128, NCH * 256], BF16)) if layer == 1 else None
                cbt = dsc[:, 0:256] if layer == 1 else None
                bbt = dsc[:, 256:512] if layer == 1 else None

                gps = pf.enter_context(nc.psum_tensor("f_gps" + ltag, [128, 2 * 512], F32))
                gps2 = pf.enter_context(nc.psum_tensor("f_gps2" + ltag, [128, 2 * 512], F32))
                ups = pf.enter_context(nc.psum_tensor("f_ups" + ltag, [128, 2 * 512], F32)) \
                    if layer == 0 else gps2
                trp = pf.enter_context(nc.psum_tensor("f_trp" + ltag, [128, 1024], BF16))
                psT = pf.enter_context(nc.psum_tensor("f_psT" + ltag, [128, 512], F32))
                bcp = pf.enter_context(nc.psum_tensor("f_bcp" + ltag, [128, 512], F32)) \
                    if layer == 1 else None
                gpar = (gps, gps2)

                wA_d = wgT_d if layer == 0 else w1T_d
                wB_d = wuT_d if layer == 0 else None
                wD_d = wdT_d if layer == 0 else w2T_d
                mA = MG if layer == 0 else M1
                mD = MD if layer == 0 else M2

                for g in range(NCH):
                    subs = (2 * g, 2 * g + 1)
                    if layer == 0:
                        for k, s in enumerate(subs):
                            pb.tt(tcf[:, 4 * k + 0:4 * k + 1], invs_l0[:, s:s + 1], wmean2[:, MG:MG + 1], ALU.mult)
                            pb.tt(tcf[:, 4 * k + 1:4 * k + 2], invs_l0[:, s:s + 1], wmean2[:, MU:MU + 1], ALU.mult)
                    else:
                        t_bc = pb.mm(bcp[:, 0:256], ones_rf[0:1, :], c1row[0:1, 256 * g:256 * (g + 1)], inc=("TE", 1))
                        t_bc = pb.mm(bcp[:, 256:512], ones_rf[0:1, :], m1row[0:1, 256 * g:256 * (g + 1)], inc=("TE", 1))
                        pb.wait("vector", "TE", t_bc)
                        pb.copy(cbt[:, :], bcp[:, 0:256])
                        pb.copy(bbt[:, :], bcp[:, 256:512])
                        pb.stt(bbt[:, :], bbt[:, :], -1.0, cbt[:, :], ALU.mult, ALU.mult)
                        v_cb = pb.ts(bbt[:, :], bbt[:, :], 0.5, None, op0=ALU.add, inc=("VE", 1))
                        pb.wait("tensor", "VE", v_cb)
                        for c in range(NCH):
                            x1c = x1acc[:, S * c + 256 * g:S * c + 256 * (g + 1)]
                            pb.tt(wq5[:, 0:256], x1c, cbt[:, :], ALU.mult)
                            pb.tt(wq5[:, 0:256], wq5[:, 0:256], bbt[:, :], ALU.add)
                            round_chain(wq5[:, 0:256], Ri5[:, 0:256], d5[:, 0:256], s15[:, 0:256],
                                        s25[:, 0:256], t125[:, 0:256], xqL[:, 256 * c:256 * (c + 1)],
                                        wide="mid")
                        for k, s in enumerate(subs):
                            pb.tt(tcf[:, 4 * k + 0:4 * k + 1], s1col_all[:, s:s + 1], wmean2[:, M1:M1 + 1], ALU.mult)
                    v_sc = pb.mark("vector")
                    pb.wait("scalar", "VE", v_sc)
                    pb.wait("tensor", "VE", v_sc)

                    passes = (("g", wA_d), ("u", wB_d)) if layer == 0 else (("g", wA_d),)
                    steps = [(oc, pname, wdd) for oc in range(NCH) for pname, wdd in passes]
                    ldv = {}
                    t_of = {}
                    evac_gate = {}
                    v_h_last = None

                    def issue_load(si):
                        oc_, _pn, wdd_ = steps[si]
                        half = (si % 2) * 4096
                        ldv[si] = pb.dma(
                            wbufA[:, half:half + 4096].rearrange("p (j i q) -> p j i q", j=NCH, i=4),
                            wdd_[oc_, :, :, :, :], eng="gpsimd")
                    pb.wait("gpsimd", "TE", pb.cnt["TE"])
                    issue_load(0)
                    t_mm = None
                    for si, (oc, pname, wdd) in enumerate(steps):
                        half = (si % 2) * 4096
                        pb.wait("tensor", "GD", ldv[si])
                        if si + 1 < len(steps):
                            if si >= 1:
                                pb.wait("gpsimd", "TE", t_of[si - 1])
                            issue_load(si + 1)
                        if pname == "g":
                            dst = gpar[oc % 2]
                            if (oc % 2) in evac_gate:
                                pb.wait("tensor", "SE", evac_gate[oc % 2])
                        else:
                            dst = ups
                            if v_h_last is not None:
                                pb.wait("tensor", "VE", v_h_last)
                        for k, s in enumerate(subs):
                            for j in range(NCH):
                                if layer == 0:
                                    lt = xq1T[:, S * j + 128 * s:S * j + 128 * (s + 1)]
                                else:
                                    lt = xqL[:, 256 * j + 128 * k:256 * j + 128 * (k + 1)]
                                pb.mm(dst[:, 512 * k:512 * (k + 1)], lt,
                                      wbufA[:, half + 512 * j:half + 512 * (j + 1)],
                                      start=(j == 0), stop=(j == NCH - 1 and layer == 0))
                            if layer == 1:
                                pb.mm(dst[:, 512 * k:512 * (k + 1)], i1row[0:1, 128 * s:128 * (s + 1)],
                                      b1row[0:1, 512 * oc:512 * (oc + 1)], start=False, stop=True)
                        t_mm = t_of[si] = pb.mark("tensor")
                        if pname != passes[-1][0]:
                            continue
                        pb.wait("scalar", "TE", t_mm)
                        pb.wait("vector", "TE", t_mm)
                        v_h = None
                        for k, s in enumerate(subs):
                            hsl = h_sb[:, INNER * k + 512 * oc:INNER * k + 512 * (oc + 1)]
                            if layer == 0:
                                if v_h is not None:
                                    pb.wait("scalar", "VE", v_h)
                                s_g = pb.act(sgt[:, :], gpar[oc % 2][:, 512 * k:512 * (k + 1)], AF.Silu,
                                             scale=tcf[:, 4 * k:4 * k + 1], inc=("SE", 1))
                                pb.wait("vector", "SE", s_g)
                                v_h = pb.stt(hsl, ups[:, 512 * k:512 * (k + 1)], tcf[:, 4 * k + 1:4 * k + 2],
                                             sgt[:, :], ALU.mult, ALU.mult, inc=("VE", 1))
                            else:
                                v_h = pb.act(hsl, gpar[oc % 2][:, 512 * k:512 * (k + 1)], AF.Gelu,
                                             scale=tcf[:, 4 * k:4 * k + 1], inc=("SE", 1))
                        if layer == 0:
                            evac_gate[oc % 2] = s_g
                            v_h_last = v_h
                        else:
                            evac_gate[oc % 2] = v_h
                    if layer == 1:
                        pb.wait("vector", "SE", v_h)
                        pb.wait("tensor", "SE", v_h)
                    else:
                        pb.wait("tensor", "VE", v_h)

                    dld = {}

                    def issue_dload(jj):
                        dh = (jj % 2) * 1024
                        dld[jj] = pb.dma(wbufD[:, dh:dh + 1024],
                                         wD_d[jj, :, :, :].rearrange("p i q -> p (i q)"), eng="gpsimd")
                    pb.wait("gpsimd", "TE", pb.cnt["TE"])
                    issue_dload(0)

                    for k, s in enumerate(subs):
                        hrow = h_sb[:, INNER * k:INNER * (k + 1)]
                        pb.red(tcf[:, 4 * k + 2:4 * k + 3], hrow, ALU.max, absval=True)
                        if layer == 1:
                            for oc in range(NCH):
                                hsl = h_sb[:, INNER * k + 512 * oc:INNER * k + 512 * (oc + 1)]
                                pb.stt(junk5[:, :], hsl, 1.0, hsl, ALU.mult, ALU.mult, accum=hsq[:, oc:oc + 1])
                            pb.red(tcf[:, 4 * k + 3:4 * k + 4], hsq[:, :], ALU.add)
                            pb.ts(tcf[:, 4 * k + 3:4 * k + 4], tcf[:, 4 * k + 3:4 * k + 4], 1.0 / 4096, 1e-6,
                                  ALU.mult, ALU.add)
                            v_hs = pb.mark("vector")
                            pb.wait("scalar", "VE", v_hs)
                            pb.act(tcf[:, 4 * k + 3:4 * k + 4], tcf[:, 4 * k + 3:4 * k + 4], AF.Ln)
                            s_hs = pb.act(tcf[:, 4 * k + 3:4 * k + 4], tcf[:, 4 * k + 3:4 * k + 4], AF.Exp,
                                          scale=-0.5, inc=("SE", 1))
                            pb.wait("vector", "SE", s_hs)
                            pb.tt(tcf[:, 4 * k + 2:4 * k + 3], tcf[:, 4 * k + 2:4 * k + 3],
                                  tcf[:, 4 * k + 3:4 * k + 4], ALU.mult)
                        pb.ts(tcf[:, 4 * k + 2:4 * k + 3], tcf[:, 4 * k + 2:4 * k + 3], 1e-5, None, op0=ALU.max)
                        pb.recip(tcf[:, 8 + k:9 + k], tcf[:, 4 * k + 2:4 * k + 3])
                        pb.ts(tcf[:, 8 + k:9 + k], tcf[:, 8 + k:9 + k], 127.0, None)
                        pb.recip(tcf[:, 10 + k:11 + k], tcf[:, 8 + k:9 + k])
                        if layer == 1:
                            pb.tt(tcf[:, 8 + k:9 + k], tcf[:, 8 + k:9 + k], tcf[:, 4 * k + 3:4 * k + 4], ALU.mult)
                        pb.tt(tcf[:, 12 + k:13 + k], tcf[:, 10 + k:11 + k], wmean2[:, mD:mD + 1], ALU.mult)
                    v_hsc = pb.mark("vector")
                    pb.wait("scalar", "VE", v_hsc)

                    for k, s in enumerate(subs):
                        for oc8 in range(4):
                            hsl = h_sb[:, INNER * k + 1024 * oc8:INNER * k + 1024 * (oc8 + 1)]
                            s_w5 = pb.act(wq5[:, :], hsl, AF.Identity, scale=tcf[:, 8 + k:9 + k],
                                          bias=cc[:, CHALF:CHALF + 1], inc=("SE", 1))
                            pb.wait("vector", "SE", s_w5)
                            round_chain(wq5[:, :], Ri5[:, :], d5[:, :], s15[:, :], s25[:, :], t125[:, :], rq5[:, :],
                                        wide=True)
                            v_rq = pb.mark("vector")
                            pb.wait("tensor", "VE", v_rq)
                            for h2 in range(2):
                                tb = 512 * h2
                                t_tq = None
                                for b in range(4):
                                    t_tq = pb.tr(trp[:, tb + 128 * b:tb + 128 * (b + 1)],
                                                 rq5[:, tb + 128 * b:tb + 128 * (b + 1)], ident[:, :],
                                                 inc=("TE", 1) if b == 3 else None)
                                pb.wait("vector", "TE", t_tq)
                                jj0 = 8 * oc8 + 4 * h2
                                dstv = hqT[:, 256 * jj0:256 * (jj0 + 4)].rearrange(
                                    "p (j t2) -> p j t2", j=4)[:, :, 128 * k:128 * (k + 1)]
                                v_hq = pb.copy(dstv, trp[:, tb:tb + 512].rearrange("p (j t) -> p j t", j=4),
                                               inc=("VE", 1))
                                pb.wait("tensor", "VE", v_hq)
                            pb.wait("scalar", "VE", v_hq)

                    if layer == 1:
                        for k, s in enumerate(subs):
                            pb.recip(tcf[:, 14 + k:15 + k], tcf[:, 12 + k:13 + k])
                            v_i2 = pb.mark("vector")
                            pb.wait("tensor", "VE", v_i2)
                            t_i2 = pb.tr(psT[0:1, 0:128], tcf[:, 14 + k:15 + k], identf[:, :], inc=("TE", 1))
                            pb.wait("vector", "TE", t_i2)
                            pb.copy(rowb[0:1, 128 * k:128 * (k + 1)], psT[0:1, 0:128])
                        v_i2b = pb.mark("vector")
                        pb.wait("tensor", "VE", v_i2b)

                    if DEBUG_TAPS and sq == 0 and layer == 0 and g == DEBUG_TAP_GROUP:
                        pb.barrier()
                        pb.dma(dbg_h_d[:, :], h_sb[:, :], store=True)
                        pb.barrier()
                        pb.dma(dbg_tcf_d[:, 0:3], tcf[:, 0:3], store=True)
                        pb.barrier()
                        pb.dma(dbg_tcf_d[:, 4:7], tcf[:, 4:7], store=True)
                        pb.barrier()
                        pb.dma(dbg_tcf_d[:, 8:14], tcf[:, 8:14], store=True)
                        pb.barrier()
                        pb.dma(dbg_hqT_d[:, :], hqT[:, :], store=True)
                        pb.barrier()

                    def dreg(k, half):
                        t = gps if k == 0 else ups
                        return t[:, 512 * half:512 * (half + 1)]
                    t_dof = {}
                    for jj in range(ICH):
                        dhalf = (jj % 2) * 1024
                        pb.wait("tensor", "GD", dld[jj])
                        if jj + 1 < ICH:
                            if jj >= 1:
                                pb.wait("gpsimd", "TE", t_dof[jj - 1])
                            issue_dload(jj + 1)
                        for k in range(2):
                            for half in range(2):
                                pb.mm(dreg(k, half), hqT[:, 256 * jj + 128 * k:256 * jj + 128 * (k + 1)],
                                      wbufD[:, dhalf + 512 * half:dhalf + 512 * (half + 1)],
                                      start=(jj == 0), stop=(jj == ICH - 1 and layer == 0))
                        t_dof[jj] = pb.mark("tensor")
                    if layer == 1:
                        for k, s in enumerate(subs):
                            for half in range(2):
                                pb.mm(dreg(k, half), rowb[0:1, 128 * k:128 * (k + 1)],
                                      b2row[0:1, 512 * half:512 * (half + 1)], start=False, stop=True)
                    t_dn = pb.mark("tensor")
                    pb.wait("scalar", "TE", t_dn)
                    for k, s in enumerate(subs):
                        for half in range(2):
                            s_dc = pb.act(dsc[:, :], dreg(k, half), AF.Identity, scale=tcf[:, 12 + k:13 + k],
                                          inc=("SE", 1))
                            pb.wait("tensor", "SE", s_dc)
                            t_dt = None
                            for b in range(4):
                                t_dt = pb.tr(psT[:, 128 * b:128 * (b + 1)], dsc[:, 128 * b:128 * (b + 1)],
                                             identf[:, :], inc=("TE", 1) if b == 3 else None)
                            pb.wait("vector", "TE", t_dt)
                            xsl = x1acc[:, :].rearrange("p (c t) -> p c t", c=NCH)[
                                :, 4 * half:4 * (half + 1), 128 * s:128 * (s + 1)]
                            v_dn = pb.stt(xsl, psT[:, :].rearrange("p (b t) -> p b t", b=4), 1.0, xsl,
                                          ALU.mult, ALU.add, inc=("VE", 1))
                            pb.wait("tensor", "VE", v_dn)
                            pb.wait("scalar", "VE", v_dn)

        if DEBUG_TAPS and sq == 0:
            pb.dma(dbg_xq_d[:, :], xq1T[:, :], store=True)
            pb.barrier()
        emit_ffn(0)
        pb.barrier()
        if DEBUG_TAPS and sq == 0:
            pb.dma(dbg1_d[:, :], x1acc[:, :], store=True)
            pb.barrier()
        xq_es.__exit__(None, None, None)
        kvz_es = contextlib.ExitStack()
        kvz_es.__enter__()
        kvz_sb = kvz_es.enter_context(sb("kvz" + tag, [128, H * 65], BF16))

        with contextlib.ExitStack() as p2:
            wkrf_sb = p2.enter_context(sb("a_wkrf" + tag, [128, NCH * 2048], BF16))
            wv_sb = p2.enter_context(sb("a_wv" + tag, [128, NCH * 1024], BF16))
            kp_sb = p2.enter_context(sb("a_kp" + tag, [128, 2048], BF16))
            vv_sb = p2.enter_context(sb("a_vv" + tag, [128, H * 65], BF16))
            ytile = p2.enter_context(sb("a_y" + tag, [128, 512], F32))
            n1t = p2.enter_context(sb("a_n1" + tag, [128, 512], BF16))
            kvzp = p2.enter_context(nc.psum_tensor("a_kvzp" + tag, [128, H * 128], F32))
            pp1 = p2.enter_context(nc.psum_tensor("a_pp1" + tag, [128, 512], F32))
            pp2 = p2.enter_context(nc.psum_tensor("a_pp2" + tag, [128, 512], F32))

            pb.dma(wkrf_sb[:, :].rearrange("p (i h q) -> p i h q", i=NCH, h=H),
                   wkrf_d[0:NCH, :, :, :].rearrange("i p h q -> p i h q"))
            ld_wv = pb.dma(wv_sb[:, :].rearrange("p (i q) -> p i q", i=NCH),
                           dt_in["L0_attn_Wv"][:, :].rearrange("(i p) q -> p i q", p=128), eng="gpsimd")
            pb.wait("gpsimd", "GD", ld_wv)
            pb.memset(vv_sb[:, :].rearrange("p (h q) -> p h q", h=H)[:, :, 64:65], 1.0)
            g_on = pb.mark("gpsimd")
            pb.wait("tensor", "GE", g_on)
            pb.wait("tensor", "DL", pb.cnt["DL"])
            pb.wait("tensor", "GD", pb.cnt["GD"])

            for s in range(NSUB):
                t_v = None
                for half in range(2):
                    pp = pp1 if half == 0 else pp2
                    for j in range(NCH):
                        t_v = pb.mm(pp[:, :], xnbf[:, S * j + 128 * s:S * j + 128 * (s + 1)],
                                    wv_sb[:, 1024 * j + 512 * half:1024 * j + 512 * (half + 1)],
                                    start=(j == 0), stop=(j == NCH - 1),
                                    inc=("TE", 1) if (half == 1 and j == NCH - 1) else None)
                pb.wait("vector", "TE", t_v)
                vvv = vv_sb[:, :].rearrange("p (h q) -> p h q", h=H)[:, :, 0:64]
                pb.copy(vvv[:, 0:8, :], pp1[:, :].rearrange("p (h q) -> p h q", h=8))
                v_v = pb.copy(vvv[:, 8:16, :], pp2[:, :].rearrange("p (h q) -> p h q", h=8), inc=("VE", 1))
                pb.wait("tensor", "VE", v_v)
                for fs in range(4):
                    pp = pp1 if fs % 2 == 0 else pp2
                    t_k = None
                    for j in range(NCH):
                        t_k = pb.mm(pp[:, :], xnbf[:, S * j + 128 * s:S * j + 128 * (s + 1)],
                                    wkrf_sb[:, 2048 * j + 512 * fs:2048 * j + 512 * (fs + 1)],
                                    start=(j == 0), stop=(j == NCH - 1),
                                    inc=("TE", 1) if j == NCH - 1 else None)
                    pb.wait("scalar", "TE", t_k)
                    ppv = pp[:, :].rearrange("p (h q) -> p h q", h=4)
                    ytv = ytile[:, :].rearrange("p (h q) -> p h q", h=4)
                    pb.act(ytv[:, :, 0:64], ppv[:, :, 0:64], AF.Identity, bias=cc[:, CPIHALL:CPIHALL + 1])
                    s_y = pb.act(ytv[:, :, 64:128], ppv[:, :, 64:128], AF.Identity, inc=("SE", 1))
                    pb.wait("vector", "SE", s_y)
                    s_k = wrap_sin(ytile[:, :], n1t[:, :], kp_sb[:, 512 * fs:512 * (fs + 1)])
                    pb.wait("tensor", "SE", s_k)
                t_kv = None
                for h in range(H):
                    t_kv = pb.mm(kvzp[:, 128 * h:128 * h + 65], kp_sb[:, 128 * h:128 * (h + 1)],
                                 vv_sb[:, 65 * h:65 * (h + 1)],
                                 start=(s == 0 and h % 4 == 0), stop=(s == NSUB - 1 and h % 4 == 3),
                                 inc=("TE", 1) if h == H - 1 else None)
                pb.wait("vector", "TE", t_kv)
                pb.wait("scalar", "TE", t_kv)
            kvv = kvzp[:, :].rearrange("p (h q) -> p h q", h=H)[:, :, 0:65]
            v_kvz = pb.copy(kvz_sb[:, :].rearrange("p (h q) -> p h q", h=H), kvv, inc=("VE", 1))
            pb.wait("tensor", "VE", v_kvz)
        pb.barrier()

        with contextlib.ExitStack() as p4:
            wqrf_sb = p4.enter_context(sb("b_wqrf" + tag, [128, NCH * 2048], BF16))
            wo_sb = p4.enter_context(sb("b_wo" + tag, [128, NCH * 1024], BF16))
            qpT = p4.enter_context(sb("b_qp" + tag, [128, 512], BF16))
            ho_sb = p4.enter_context(sb("b_ho" + tag, [128, 4 * 1024], BF16))
            hoT = p4.enter_context(sb("b_hoT" + tag, [128, NCH * 512], BF16))
            ytile = p4.enter_context(sb("b_y" + tag, [128, 512], F32))
            n1t = p4.enter_context(sb("b_n1" + tag, [128, 512], BF16))
            zcol = p4.enter_context(sb("b_z" + tag, [128, 4], F32))
            qq = p4.enter_context(nc.psum_tensor("b_qq" + tag, [128, 512], F32))
            qq2 = p4.enter_context(nc.psum_tensor("b_qq2" + tag, [128, 512], F32))
            zp = p4.enter_context(nc.psum_tensor("b_zp" + tag, [128, 4 * 128], F32))
            trb = p4.enter_context(nc.psum_tensor("b_trb" + tag, [128, 1024], BF16))
            wop = p4.enter_context(nc.psum_tensor("b_wop" + tag, [128, 512], F32))

            pb.dma(wqrf_sb[:, :].rearrange("p (i h q) -> p i h q", i=NCH, h=H),
                   wqrf_d[0:NCH, :, :, :].rearrange("i p h q -> p i h q"))
            pb.dma(wo_sb[:, :].rearrange("p (i q) -> p i q", i=NCH),
                   dt_in["L0_attn_Wo"][:, :].rearrange("(i p) q -> p i q", p=128), eng="gpsimd")
            pb.wait("tensor", "DL", pb.cnt["DL"])
            pb.wait("tensor", "GD", pb.cnt["GD"])

            for g in range(4):
                for h in range(H):
                    pp = qq if h % 2 == 0 else qq2
                    t_q = None
                    for j in range(NCH):
                        t_q = pb.mm(pp[:, :], wqrf_sb[:, 2048 * j + 128 * h:2048 * j + 128 * (h + 1)],
                                    xnbf[:, S * j + 512 * g:S * j + 512 * (g + 1)],
                                    start=(j == 0), stop=(j == NCH - 1),
                                    inc=("TE", 1) if j == NCH - 1 else None)
                    pb.wait("scalar", "TE", t_q)
                    s_y = pb.act(ytile[:, :], pp[:, :], AF.Identity, bias=cc[:, CPIH2:CPIH2 + 1], inc=("SE", 1))
                    pb.wait("vector", "SE", s_y)
                    s_k = wrap_sin(ytile[:, :], n1t[:, :], qpT[:, :])
                    pb.wait("tensor", "SE", s_k)
                    t_z = None
                    for st in range(4):
                        t_z = pb.mm(zp[:, 128 * st:128 * st + 65], qpT[:, 128 * st:128 * (st + 1)],
                                    kvz_sb[:, 65 * h:65 * (h + 1)],
                                    inc=("TE", 1) if st == 3 else None)
                    pb.wait("vector", "TE", t_z)
                    for st in range(4):
                        pb.ts(zcol[:, st:st + 1], zp[:, 128 * st + 64:128 * st + 65], ZCLIP, None, op0=ALU.max)
                    v_z = pb.mark("vector")
                    pb.recip(zcol[:, 0:4], zcol[:, 0:4])
                    v_z = pb.mark("vector")
                    pb.wait("scalar", "VE", v_z)
                    s_ho = None
                    for st in range(4):
                        s_ho = pb.act(ho_sb[:, 1024 * st + 64 * h:1024 * st + 64 * (h + 1)],
                                      zp[:, 128 * st:128 * st + 64], AF.Identity, scale=zcol[:, st:st + 1],
                                      inc=("SE", 1) if st == 3 else None)
                    pb.wait("tensor", "SE", s_ho)
                for st in range(4):
                    t_h = None
                    for c in range(NCH):
                        t_h = pb.tr(trb[:, 128 * c:128 * (c + 1)],
                                    ho_sb[:, 1024 * st + 128 * c:1024 * st + 128 * (c + 1)], ident[:, :],
                                    inc=("TE", 1) if c == NCH - 1 else None)
                    pb.wait("vector", "TE", t_h)
                    hv = hoT[:, :].rearrange("p (c t) -> p c t", c=NCH)[:, :, 128 * st:128 * (st + 1)]
                    v_h2 = pb.copy(hv, trb[:, :].rearrange("p (c t) -> p c t", c=NCH), inc=("VE", 1))
                    pb.wait("tensor", "VE", v_h2)
                for oc in range(NCH):
                    t_w = None
                    for j in range(NCH):
                        t_w = pb.mm(wop[:, :], wo_sb[:, 1024 * j + 128 * oc:1024 * j + 128 * (oc + 1)],
                                    hoT[:, 512 * j:512 * (j + 1)],
                                    start=(j == 0), stop=(j == NCH - 1),
                                    inc=("TE", 1) if j == NCH - 1 else None)
                    pb.wait("vector", "TE", t_w)
                    xsl = x1acc[:, S * oc + 512 * g:S * oc + 512 * (g + 1)]
                    v_w = pb.stt(xsl, wop[:, :], 1.0, xsl, ALU.mult, ALU.add, inc=("VE", 1))
                    pb.wait("tensor", "VE", v_w)
        pb.barrier()

        if DEBUG_TAPS and sq == 0:
            pb.dma(dbg2_d[:, :], x1acc[:, :], store=True)
            pb.dma(dbg3_d[:, :], kvz_sb[:, :], store=True)
            pb.barrier()
        kvz_es.__exit__(None, None, None)

        prow_es = contextlib.ExitStack()
        prow_es.__enter__()
        m1row = prow_es.enter_context(sb("m1row" + tag, [1, S], F32))
        c1row = prow_es.enter_context(sb("c1row" + tag, [1, S], F32))
        i1row = prow_es.enter_context(sb("i1row" + tag, [1, S], BF16))
        s1col_all = prow_es.enter_context(sb("s1col" + tag, [128, NSUB], F32))
        attn1_es = contextlib.ExitStack()
        attn1_es.__enter__()
        qT_t = attn1_es.enter_context(sb("qT_t" + tag, [64, S], BF16))
        kT_t = attn1_es.enter_context(sb("kT_t" + tag, [64, S], BF16))
        v1_sb = attn1_es.enter_context(sb("v1_sb" + tag, [128, NSUB * 64], BF16))
        ohT = attn1_es.enter_context(sb("ohT" + tag, [64, S], BF16))
        r1_es = contextlib.ExitStack()
        r1_es.__enter__()
        r1row = r1_es.enter_context(sb("r1row" + tag, [1, S], F32))
        with contextlib.ExitStack() as p5:
            sqt = p5.enter_context(sb("p5_sq" + tag, [128, 512], BF16))
            runmx = p5.enter_context(sb("p5_mx" + tag, [128, S], F32))
            runmn = p5.enter_context(sb("p5_mn" + tag, [128, S], F32))
            colsc = p5.enter_context(sb("p5_col" + tag, [128, 4], F32))
            rowA = p5.enter_context(sb("p5_ra" + tag, [1, S], F32))
            rowB = p5.enter_context(sb("p5_rb" + tag, [1, S], F32))
            mp = p5.enter_context(nc.psum_tensor("p5_mp" + tag, [1, 512], F32))
            qp = p5.enter_context(nc.psum_tensor("p5_qp" + tag, [1, 512], F32))
            tp = p5.enter_context(nc.psum_tensor("p5_tp" + tag, [128, 256], F32))

            v_cast = None
            for c in range(NCH):
                v_cast = pb.copy(xnbf[:, S * c:S * (c + 1)], x1acc[:, S * c:S * (c + 1)],
                                 inc=("VE", 1) if c == NCH - 1 else None)
            pb.wait("tensor", "VE", v_cast)
            pb.wait("scalar", "VE", v_cast)
            for tsl in range(4):
                t_m = None
                for c in range(NCH):
                    sl = xnbf[:, S * c + 512 * tsl:S * c + 512 * (tsl + 1)]
                    s_sq = pb.act(sqt[:, :], sl, AF.Square, inc=("SE", 1))
                    pb.mm(mp[:, :], ones_c[:, :], sl, start=(c == 0), stop=(c == NCH - 1))
                    pb.wait("tensor", "SE", s_sq)
                    t_m = pb.mm(qp[:, :], ones_c[:, :], sqt[:, :], start=(c == 0), stop=(c == NCH - 1),
                                inc=("TE", 1))
                    pb.wait("scalar", "TE", t_m)
                pb.wait("scalar", "TE", t_m)
                s_ev = pb.act(m1row[0:1, 512 * tsl:512 * (tsl + 1)], mp[:, :], AF.Identity, scale=1.0 / 1024)
                s_ev = pb.act(rowA[0:1, 512 * tsl:512 * (tsl + 1)], qp[:, :], AF.Identity, scale=1.0 / 1024,
                              inc=("SE", 1))
                pb.wait("tensor", "SE", s_ev)
                pb.wait("vector", "SE", s_ev)
            pb.tt(rowB[0:1, :], m1row[0:1, :], m1row[0:1, :], ALU.mult)
            pb.tt(rowA[0:1, :], rowA[0:1, :], rowB[0:1, :], ALU.subtract)
            pb.ts(rowB[0:1, :], rowA[0:1, :], 1e-5, None, op0=ALU.add)
            v_rows = pb.mark("vector")
            pb.wait("scalar", "VE", v_rows)
            pb.act(r1row[0:1, :], rowB[0:1, :], AF.Ln)
            s_rw = pb.act(r1row[0:1, :], r1row[0:1, :], AF.Exp, scale=-0.5, inc=("SE", 1))
            pb.wait("vector", "SE", s_rw)
            pb.recip(rowB[0:1, :], rowB[0:1, :])
            pb.tt(rowA[0:1, :], rowA[0:1, :], rowB[0:1, :], ALU.mult)
            pb.ts(rowA[0:1, :], rowA[0:1, :], 1e-6, None, op0=ALU.add)
            v_rw2 = pb.mark("vector")
            pb.wait("scalar", "VE", v_rw2)
            pb.act(rowA[0:1, :], rowA[0:1, :], AF.Ln)
            s_rw2 = pb.act(rowA[0:1, :], rowA[0:1, :], AF.Exp, scale=-0.5, inc=("SE", 1))
            pb.wait("vector", "SE", s_rw2)
            pb.copy(runmx[:, :], x1acc[:, 0:S])
            pb.copy(runmn[:, :], x1acc[:, 0:S])
            for c in range(1, NCH):
                pb.tt(runmx[:, :], runmx[:, :], x1acc[:, S * c:S * (c + 1)], ALU.max)
                pb.tt(runmn[:, :], runmn[:, :], x1acc[:, S * c:S * (c + 1)], ALU.min)
            v_tree = pb.mark("vector")
            pb.wait("tensor", "VE", v_tree)
            for s in range(NSUB):
                pb.tr(tp[:, 0:128], runmx[:, 128 * s:128 * (s + 1)], identf[:, :])
                t_t = pb.tr(tp[:, 128:256], runmn[:, 128 * s:128 * (s + 1)], identf[:, :], inc=("TE", 1))
                pb.wait("vector", "TE", t_t)
                pb.red(colsc[:, 0:1], tp[:, 0:128], ALU.max)
                v_c0 = pb.red(colsc[:, 1:2], tp[:, 128:256], ALU.min, inc=("VE", 1))
                pb.wait("tensor", "VE", v_c0)
                t_mc = pb.tr(tp[:, 0:1], m1row[0:1, 128 * s:128 * (s + 1)], identf[0:1, 0:1], inc=("TE", 1))
                pb.wait("vector", "TE", t_mc)
                pb.tt(colsc[:, 0:1], colsc[:, 0:1], tp[:, 0:1], ALU.subtract)
                pb.tt(colsc[:, 1:2], tp[:, 0:1], colsc[:, 1:2], ALU.subtract)
                v_c1 = pb.tt(colsc[:, 2:3], colsc[:, 0:1], colsc[:, 1:2], ALU.max, inc=("VE", 1))
                pb.wait("tensor", "VE", v_c1)
                t_mx = pb.tr(tp[0:1, 0:128], colsc[:, 2:3], identf[:, :], inc=("TE", 1))
                pb.wait("vector", "TE", t_mx)
                v_mx = pb.copy(rowB[0:1, 128 * s:128 * (s + 1)], tp[0:1, 0:128], inc=("VE", 1))
                pb.wait("tensor", "VE", v_mx)
            pb.tt(rowB[0:1, :], rowB[0:1, :], r1row[0:1, :], ALU.mult)
            pb.tt(rowB[0:1, :], rowB[0:1, :], rowA[0:1, :], ALU.mult)
            pb.ts(rowB[0:1, :], rowB[0:1, :], 1e-5, None, op0=ALU.max)
            pb.recip(rowB[0:1, :], rowB[0:1, :])
            pb.ts(rowB[0:1, :], rowB[0:1, :], 127.0, None)
            pb.tt(c1row[0:1, :], r1row[0:1, :], rowA[0:1, :], ALU.mult)
            pb.tt(c1row[0:1, :], c1row[0:1, :], rowB[0:1, :], ALU.mult)
            pb.recip(rowA[0:1, 0:1], wmean2[0:1, M1:M1 + 1])
            pb.op("vector", lambda e, _o=i1row[0:1, :], _i=rowB[0:1, :], _s=rowA[0:1, 0:1]:
                  e.tensor_single_scalar(out=_o, in_=_i, scalar=_s, op=ALU.mult),
                  reads=[rowB[0:1, :], rowA[0:1, 0:1]], writes=[i1row[0:1, :]])
            pb.recip(rowA[0:1, :], rowB[0:1, :])
            v_inv = pb.mark("vector")
            pb.wait("tensor", "VE", v_inv)
            for s in range(NSUB):
                t_iv = pb.tr(tp[:, 0:1], rowA[0:1, 128 * s:128 * (s + 1)], identf[0:1, 0:1], inc=("TE", 1))
                pb.wait("vector", "TE", t_iv)
                v_iv = pb.copy(s1col_all[:, s:s + 1], tp[:, 0:1], inc=("VE", 1))
                pb.wait("tensor", "VE", v_iv)
        pb.barrier()

        with contextlib.ExitStack() as p6:
            mbt = p6.enter_context(sb("p6_mb" + tag, [128, 512], F32))
            rbt = p6.enter_context(sb("p6_rb" + tag, [128, 512], F32))
            scr = p6.enter_context(sb("p6_scr" + tag, [128, 512], F32))
            bp = p6.enter_context(nc.psum_tensor("p6_bp" + tag, [128, 1024], F32))
            qkp = p6.enter_context(nc.psum_tensor("p6_qkp" + tag, [128, 512], F32))
            vp = p6.enter_context(nc.psum_tensor("p6_vp" + tag, [128, 128], F32))

            for g in range(4):
                t_b = pb.mm(bp[:, 0:512], ones_rf[0:1, :], m1row[0:1, 512 * g:512 * (g + 1)], inc=("TE", 1))
                t_b = pb.mm(bp[:, 512:1024], ones_rf[0:1, :], r1row[0:1, 512 * g:512 * (g + 1)], inc=("TE", 1))
                pb.wait("vector", "TE", t_b)
                pb.copy(mbt[:, :], bp[:, 0:512])
                v_b = pb.copy(rbt[:, :], bp[:, 512:1024], inc=("VE", 1))
                pb.wait("tensor", "VE", v_b)
                v_xn = None
                for c in range(NCH):
                    xsl = x1acc[:, S * c + 512 * g:S * c + 512 * (g + 1)]
                    pb.tt(scr[:, :], xsl, mbt[:, :], ALU.subtract)
                    v_xn = pb.tt(xnbf[:, S * c + 512 * g:S * c + 512 * (g + 1)], scr[:, :], rbt[:, :], ALU.mult,
                                 inc=("VE", 1) if c == NCH - 1 else None)
                pb.wait("tensor", "VE", v_xn)
                t_qk = None
                for j in range(NCH):
                    t_qk = pb.mm(qkp[:, :], qkvT_sb[:, 192 * j:192 * j + 128],
                                 xnbf[:, S * j + 512 * g:S * j + 512 * (g + 1)],
                                 start=(j == 0), stop=(j == NCH - 1), inc=("TE", 1) if j == NCH - 1 else None)
                pb.wait("scalar", "TE", t_qk)
                pb.act(qT_t[0:64, 512 * g:512 * (g + 1)], qkp[0:64, :], AF.Identity, bias=qkvb_c[0:64, 0:1])
                s_qk = pb.act(kT_t[0:64, 512 * g:512 * (g + 1)], qkp[64:128, :], AF.Identity,
                              bias=qkvb_c[64:128, 0:1], inc=("SE", 1))
                pb.wait("tensor", "SE", s_qk)
                for kk in range(4):
                    s = 4 * g + kk
                    for j in range(NCH):
                        pb.mm(vp[:, 0:64], xnbf[:, S * j + 128 * s:S * j + 128 * (s + 1)],
                              qkvT_sb[:, 192 * j + 128:192 * (j + 1)], start=(j == 0), stop=False)
                    t_vv = pb.mm(vp[:, 0:64], ones_r[0:1, :], qkvb_vr[0:1, :], start=False, stop=True,
                                 inc=("TE", 1))
                    pb.wait("vector", "TE", t_vv)
                    v_vv = pb.copy(v1_sb[:, 64 * s:64 * (s + 1)], vp[:, 0:64], inc=("VE", 1))
                    pb.wait("tensor", "VE", v_vv)
        pb.barrier()
        if DEBUG_TAPS and sq == 0:
            pb.dma(dbg_q_d[:, :], qT_t[0:64, :], store=True)
            pb.barrier()
            pb.dma(dbg_k_d[:, :], kT_t[0:64, :], store=True)
            pb.barrier()
            pb.dma(dbg_v_d[:, :], v1_sb[:, :], store=True)
            pb.barrier()
            pb.dma(dbg_xn_d[:, :], xnbf[:, :], store=True)
            pb.barrier()
            pb.dma(dbg_mr_d[0:1, :], m1row[0:1, :], store=True)
            pb.barrier()
            pb.dma(dbg_mr_d[1:2, :], r1row[0:1, :], store=True)
            pb.barrier()
            pb.dma(dbg_bc_d[:, :], qkvb_c[:, 0:1], store=True)
            pb.barrier()
            pb.dma(dbg_P_d[:, :], PT_sb[0:64, :], store=True)
            pb.barrier()
        r1_es.__exit__(None, None, None)

        emit_ffn(1)
        pb.barrier()
        if DEBUG_TAPS and sq == 0:
            pb.dma(dbg5_d[:, :], x1acc[:, :], store=True)
            pb.barrier()

        with contextlib.ExitStack() as p7:
            sc_sb = p7.enter_context(sb("p7_sc" + tag, [128, 2 * 128], BF16))
            oh5 = p7.enter_context(sb("p7_oh" + tag, [128, 64], BF16))
            scp = p7.enter_context(nc.psum_tensor("p7_scp" + tag, [128, 2 * 128], F32))
            ohp = p7.enter_context(nc.psum_tensor("p7_ohp" + tag, [128, 64], F32))
            trp7 = p7.enter_context(nc.psum_tensor("p7_trp" + tag, [64, 128], BF16))
            ppp = p7.enter_context(nc.psum_tensor("p7_ppp" + tag, [128, 512], F32))

            for qs in range(NSUB):
                for kt in range(qs + 1):
                    pk = kt % 2
                    t_s = pb.mm(scp[:, 128 * pk:128 * (pk + 1)], kT_t[0:64, 128 * kt:128 * (kt + 1)],
                                qT_t[0:64, 128 * qs:128 * (qs + 1)], inc=("TE", 1))
                    if kt == qs:
                        pb.wait("vector", "TE", t_s)
                        v_s = pb.tt(sc_sb[:, 128 * pk:128 * (pk + 1)], scp[:, 128 * pk:128 * (pk + 1)],
                                    trimask[:, :], ALU.mult, inc=("VE", 1))
                        pb.wait("tensor", "VE", v_s)
                    else:
                        pb.wait("scalar", "TE", t_s)
                        s_s = pb.act(sc_sb[:, 128 * pk:128 * (pk + 1)], scp[:, 128 * pk:128 * (pk + 1)],
                                     AF.Identity, inc=("SE", 1))
                        pb.wait("tensor", "SE", s_s)
                    pb.mm(ohp[:, :], sc_sb[:, 128 * pk:128 * (pk + 1)], v1_sb[:, 64 * kt:64 * (kt + 1)],
                          start=(kt == 0), stop=(kt == qs))
                t_oh = pb.mark("tensor")
                pb.wait("vector", "TE", t_oh)
                v_oh = pb.copy(oh5[:, :], ohp[:, :], inc=("VE", 1))
                pb.wait("tensor", "VE", v_oh)
                t_ot = pb.tr(trp7[0:64, :], oh5[:, :], ident[:, :], inc=("TE", 1))
                pb.wait("vector", "TE", t_ot)
                v_ot = pb.copy(ohT[0:64, 128 * qs:128 * (qs + 1)], trp7[0:64, :], inc=("VE", 1))
                pb.wait("tensor", "VE", v_ot)
            if DEBUG_TAPS and sq == 0:
                pb.barrier()
                pb.dma(dbg_oh_d[:, :], ohT[0:64, :], store=True)
                pb.barrier()
            for g in range(4):
                for oc in range(NCH):
                    t_p = pb.mm(ppp[:, :], PT_sb[0:64, 128 * oc:128 * (oc + 1)],
                                ohT[0:64, 512 * g:512 * (g + 1)], inc=("TE", 1))
                    pb.wait("vector", "TE", t_p)
                    xsl = x1acc[:, S * oc + 512 * g:S * oc + 512 * (g + 1)]
                    v_p = pb.stt(xsl, ppp[:, :], projb_c[:, oc:oc + 1], xsl, ALU.add, ALU.add, inc=("VE", 1))
                    pb.wait("tensor", "VE", v_p)
        pb.barrier()
        attn1_es.__exit__(None, None, None)
        prow_es.__exit__(None, None, None)

        with contextlib.ExitStack() as p9:
            outt = p9.enter_context(sb("p9_out" + tag, [128, D], F32))
            pso = p9.enter_context(nc.psum_tensor("p9_ps" + tag, [128, D], F32))
            for s in range(NSUB):
                t_o = None
                for c in range(NCH):
                    t_o = pb.tr(pso[:, 128 * c:128 * (c + 1)], x1acc[:, S * c + 128 * s:S * c + 128 * (s + 1)],
                                identf[:, :], inc=("TE", 1) if c == NCH - 1 else None)
                pb.wait("vector", "TE", t_o)
                v_o = pb.copy(outt[:, :], pso[:, :], inc=("VE", 1))
                pb.wait("sync", "VE", v_o)
                st_o = pb.dma(ys_d[128 * s:128 * (s + 1), :], outt[:, :], store=True)
                pb.wait("vector", "DS", st_o)
                pb.wait("tensor", "DS", st_o)
        pb.barrier()

    pb.wait("sync", "DL", pb.cnt["DL"])
    pb.wait("sync", "DS", pb.cnt["DS"])
    pb.wait("sync", "GD", pb.cnt["GD"])

    with nc.Block() as block:
        @block.sync
        def _(e):
            for fn in pb.q["sync"]:
                fn(e)

        @block.tensor
        def _(e):
            for fn in pb.q["tensor"]:
                fn(e)

        @block.vector
        def _(e):
            for fn in pb.q["vector"]:
                fn(e)

        @block.scalar
        def _(e):
            for fn in pb.q["scalar"]:
                fn(e)

        @block.gpsimd
        def _(e):
            for fn in pb.q["gpsimd"]:
                fn(e)

    es.__exit__(None, None, None)
    return nc


_NC_CACHE = None


def kernel(**inputs):
    global _NC_CACHE
    if _NC_CACHE is None:
        _NC_CACHE = build_program()
    nc = _NC_CACHE
    x = np.ascontiguousarray(np.asarray(inputs["L0_attn_x"], dtype=np.float32))
    in_maps = []
    for core in range(8):
        m = {"L0_attn_x": x[2 * core:2 * core + 2]}
        for k, v in inputs.items():
            if k != "L0_attn_x":
                m[k] = np.ascontiguousarray(np.asarray(v, dtype=np.float32))
        in_maps.append(m)
    res = run_bass_kernel_spmd(nc, in_maps, core_ids=list(range(8)))
    out = np.concatenate([r["y"] for r in res.results], axis=0)
    return out.reshape(16, S, D)
